# version 40
# baseline (speedup 1.0000x reference)
import numpy as np
import ml_dtypes
import concourse.bass as bass
import concourse.mybir as mybir
from concourse.bass_utils import run_bass_kernel_spmd

F32 = mybir.dt.float32
BF16 = mybir.dt.bfloat16
AF = mybir.ActivationFunctionType
OP = mybir.AluOpType

D = 1024
L = 2048
B_PER_CORE = 2
NCH = L // 128

SAME_ENGINE_WINDOW = 4


class _Op:
    __slots__ = ("eng", "emit", "deps", "need_inc", "is_dma", "count_after",
                 "idx", "dbg", "sem_key")

    def __init__(self, eng, emit, idx, is_dma, dbg, sem_key):
        self.eng = eng
        self.emit = emit
        self.idx = idx
        self.is_dma = is_dma
        self.dbg = dbg
        self.sem_key = sem_key
        self.deps = []
        self.need_inc = is_dma
        self.count_after = None


class Sched:

    ENGS = ("tensor", "vector", "scalar", "pool", "sync")

    def __init__(self):
        self.ops = {e: [] for e in self.ENGS}
        self.reg = {}

    def op(self, eng, emit, reads=(), writes=(), is_dma=False, dbg="",
           sem_key=None):
        o = _Op(eng, emit, len(self.ops[eng]), is_dma, dbg,
                sem_key if sem_key is not None else eng)
        deps = set()
        for r in reads:
            st = self.reg.get(r)
            if st is None:
                continue
            if st["w"] is not None:
                deps.add(st["w"])
            if r.startswith("P!"):
                for rd in st["r"]:
                    if rd.eng != eng:
                        deps.add(rd)
        for w in writes:
            st = self.reg.get(w)
            if st:
                if st["w"] is not None:
                    deps.add(st["w"])
                deps.update(st["r"])
        for d in deps:
            if d is o:
                continue
            if d.eng == eng:
                if eng in ("tensor", "sync"):
                    continue
                if o.idx - d.idx > SAME_ENGINE_WINDOW:
                    continue
            o.deps.append(d)
        for r in reads:
            st = self.reg.setdefault(r, {"w": None, "r": []})
            st["r"].append(o)
        for w in writes:
            self.reg[w] = {"w": o, "r": []}
        self.ops[eng].append(o)
        return o

    def dma(self, emit, dbg="", reads=(), writes=(), dsem="su"):
        return self.op("sync", emit, reads=reads, writes=writes,
                       is_dma=True, dbg=dbg, sem_key=dsem)

    def finalize(self):
        for e in self.ENGS:
            for o in self.ops[e]:
                for d in o.deps:
                    d.need_inc = True
        self.totals = {}
        counts = {}
        for e in self.ENGS:
            for o in self.ops[e]:
                if o.need_inc or o.is_dma:
                    counts[o.sem_key] = counts.get(o.sem_key, 0) + (
                        16 if o.is_dma else 1)
                    o.count_after = counts[o.sem_key]
        self.totals = counts
        for e in self.ENGS:
            for o in self.ops[e]:
                if o.is_dma and o.sem_key == "su":
                    o.count_after = counts.get("su", 0)

    def emit_all(self, block, sems):
        outer = self

        def run_stream(eng_name):
            def f(eng_obj):
                waited = {}
                for o in outer.ops[eng_name]:
                    need = {}
                    for d in o.deps:
                        v = d.count_after
                        assert v is not None, (o.dbg, d.dbg)
                        if need.get(d.sem_key, 0) < v:
                            need[d.sem_key] = v
                    for pk in sorted(need):
                        v = need[pk]
                        if waited.get(pk, 0) < v:
                            eng_obj.wait_ge(sems[pk], v)
                            waited[pk] = v
                    inst = o.emit(eng_obj)
                    if o.need_inc:
                        inst.then_inc(sems[o.sem_key],
                                      16 if o.is_dma else 1)
                if eng_name == "sync":
                    for k, v in sorted(outer.totals.items()):
                        if k not in outer.ENGS and v:
                            eng_obj.wait_ge(sems[k], v)
            return f

        block.sync(run_stream("sync"))
        block.tensor(run_stream("tensor"))
        block.vector(run_stream("vector"))
        block.scalar(run_stream("scalar"))
        block.gpsimd(run_stream("pool"))


def build_program(nb=B_PER_CORE, nch=NCH):
    nc = bass.Bass("TRN2", target_bir_lowering=False)
    s = Sched()

    xin = nc.dram_tensor("xin", [nb, L, D], F32, kind="ExternalInput").ap()
    out = nc.dram_tensor("out", [nb, L, D], F32, kind="ExternalOutput").ap()
    w_q = nc.dram_tensor("w_q", [D, D], F32, kind="ExternalInput").ap()
    w_k1 = nc.dram_tensor("w_k1", [D, D], F32, kind="ExternalInput").ap()
    w_v = nc.dram_tensor("w_v", [D, D], F32, kind="ExternalInput").ap()
    w_k2 = nc.dram_tensor("w_k2", [D, D], F32, kind="ExternalInput").ap()
    w_p = nc.dram_tensor("w_p", [D, D], F32, kind="ExternalInput").ap()
    w_in = nc.dram_tensor("w_in", [D, 64], F32, kind="ExternalInput").ap()
    w_out = nc.dram_tensor("w_out", [64, D], F32, kind="ExternalInput").ap()
    w1d = nc.dram_tensor("w1d", [64, 16], F32, kind="ExternalInput").ap()
    b1d = nc.dram_tensor("b1d", [64, 16], F32, kind="ExternalInput").ap()
    w2d = nc.dram_tensor("w2d", [64, 16], F32, kind="ExternalInput").ap()
    b2d = nc.dram_tensor("b2d", [1, 64], F32, kind="ExternalInput").ap()
    bind = nc.dram_tensor("bind", [1, 64], F32, kind="ExternalInput").ap()
    brows = nc.dram_tensor("brows", [6, D], F32, kind="ExternalInput").ap()
    identd = nc.dram_tensor("identd", [128, 128], BF16, kind="ExternalInput").ap()
    maskd = nc.dram_tensor("maskd", [128, 128], BF16, kind="ExternalInput").ap()

    from contextlib import ExitStack
    ctx = ExitStack()
    e = ctx.enter_context

    stage = [e(nc.sbuf_tensor(f"stage{i}", [128, 1024], F32)) for i in range(2)]
    wbf = {}
    for nm in ("q", "k1", "v", "k2", "p"):
        wbf[nm] = e(nc.sbuf_tensor(f"w{nm}_bf", [128, 8 * 1024], BF16))
    win_bf = e(nc.sbuf_tensor("win_bf", [128, 8 * 128], BF16))
    wout_bf = e(nc.sbuf_tensor("wout_bf", [128, 1024], BF16))
    w1p = e(nc.sbuf_tensor("w1p", [128, 8], F32))
    w2p = e(nc.sbuf_tensor("w2p", [128, 8], F32))
    b1eff = e(nc.sbuf_tensor("b1eff", [128, 8], F32))
    ffn_small = e(nc.sbuf_tensor("ffn_small", [128, 48], F32))
    bin_dup = e(nc.sbuf_tensor("bin_dup", [128, 1], F32))
    b2h = e(nc.sbuf_tensor("b2h", [128, 1], F32))
    epsv = e(nc.sbuf_tensor("epsv", [128, 1], F32))
    ident_sb = e(nc.sbuf_tensor("ident_sb", [128, 128], BF16))
    ident_f32 = e(nc.sbuf_tensor("ident_f32", [128, 128], F32))
    mask_sb = e(nc.sbuf_tensor("mask_sb", [128, 128], BF16))
    bqb = e(nc.sbuf_tensor("bqb", [128, D], BF16))
    bk1b = e(nc.sbuf_tensor("bk1b", [128, D], BF16))
    bvb = e(nc.sbuf_tensor("bvb", [128, D], BF16))
    bk2b = e(nc.sbuf_tensor("bk2b", [128, D], BF16))
    bpb = e(nc.sbuf_tensor("bpb", [128, D], BF16))
    boutb = e(nc.sbuf_tensor("boutb", [128, D], BF16))

    x_tok = [e(nc.sbuf_tensor(f"x_tok{i}", [128, D], F32)) for i in range(3)]
    o_tok = e(nc.sbuf_tensor("o_tok", [128, D], F32))
    xT_bf = [e(nc.sbuf_tensor(f"xT_bf{i}", [128, D], BF16)) for i in range(2)]
    q_tok = e(nc.sbuf_tensor("q_tok", [128, D], BF16))
    k_tok = [e(nc.sbuf_tensor(f"k_tok{i}", [128, D], BF16)) for i in range(2)]
    v_tok = [e(nc.sbuf_tensor(f"v_tok{i}", [128, D], BF16)) for i in range(2)]
    k2_tok = e(nc.sbuf_tensor("k2_tok", [128, D], BF16))
    ztmp = stage[0]
    ze = stage[1]
    kT_bf = [e(nc.sbuf_tensor(f"kT_bf{i}", [128, D], BF16)) for i in range(2)]
    qT_buf = [e(nc.sbuf_tensor(f"qT_buf{i}", [128, 8 * 129], BF16))
              for i in range(nb)]
    k2T_buf = [e(nc.sbuf_tensor(f"k2T_buf{i}", [128, 8 * 129], BF16))
               for i in range(nb)]
    o1T_buf = [e(nc.sbuf_tensor(f"o1T_buf{i}", [128, 8 * 129], BF16))
               for i in range(nb)]
    q2_bf = e(nc.sbuf_tensor("q2_bf", [128, D], BF16))
    e_bf = e(nc.sbuf_tensor("e_bf", [128, D], BF16))
    e_tok = e(nc.sbuf_tensor("e_tok", [128, D], BF16))
    k2s_tok = e(nc.sbuf_tensor("k2s_tok", [128, D], BF16))
    smT = [e(nc.sbuf_tensor(f"smT{i}", [128, 256], BF16)) for i in range(4)]
    yT_bf = e(nc.sbuf_tensor("yT_bf", [128, D], BF16))
    s1_f32 = [e(nc.sbuf_tensor(f"s1_f32{i}", [128, 512], F32))
              for i in range(nb)]
    s2_f32 = [e(nc.sbuf_tensor(f"s2_f32{i}", [128, 512], F32))
              for i in range(nb)]
    s1_bf = [e(nc.sbuf_tensor(f"s1_bf{i}", [128, 1024], BF16))
             for i in range(nb)]
    s2_bf = [e(nc.sbuf_tensor(f"s2_bf{i}", [128, 1024], BF16))
             for i in range(nb)]
    r1 = e(nc.sbuf_tensor("r1", [128, D], F32))
    cent = stage[1]
    r1n = [e(nc.sbuf_tensor(f"r1n{i}", [128, D], F32)) for i in range(2)]
    r1nb = e(nc.sbuf_tensor("r1nb", [128, D], F32))
    r1nT_bf = [e(nc.sbuf_tensor(f"r1nT_bf{i}", [128, D], BF16))
               for i in range(2)]
    h_all = e(nc.sbuf_tensor("h_all", [128, 1024], BF16))
    g32 = e(nc.sbuf_tensor("g32", [128, 128], F32))
    g_bf = e(nc.sbuf_tensor("g_bf", [128, 128], BF16))
    st1 = e(nc.sbuf_tensor("st1", [128, 8], F32))
    st2 = e(nc.sbuf_tensor("st2", [128, 8], F32))
    zrow = e(nc.sbuf_tensor("zrow", [1, 640], BF16))

    pp = [e(nc.psum_tensor(f"pp{i}", [128, 512], F32)) for i in range(2)]
    psb = e(nc.psum_tensor("psb", [128, 2048], F32))
    po = e(nc.psum_tensor("po", [128, 1024], F32))

    R_PP = ["P!pp0", "P!pp1"]
    R_PS = ["P!ps0", "P!ps1", "P!ps2", "P!ps3"]

    PSB_BF = psb[:, :].bitcast(BF16)
    PP_BF = [pp[i][:, :].bitcast(BF16) for i in range(2)]
    PO_BF = po[:, :].bitcast(BF16)

    def ps_bank_f32(bank):
        return psb[:, bank * 512:(bank + 1) * 512]

    def v3(t, c=129):
        return t[:, :].rearrange("p (g c) -> p g c", c=c)

    def ps_pair(par2):
        return psb[:, par2 * 1024:(par2 + 1) * 1024].rearrange(
            "p (g c) -> p g c", c=512)[:, :, 0:128]

    def po_col(p_):
        return (p_ // 4) * 512 + (p_ % 4) * 128

    def R_PO(p_):
        return "P!poA" if p_ < 4 else "P!poB"

    s.dma(lambda g: g.dma_start(out=ident_sb[:], in_=identd[:, :]),
          writes=["ident"], dbg="ld ident", dsem="sid")
    s.op("scalar", lambda sc: sc.copy(out=ident_f32[:, :], in_=ident_sb[:, :]),
         reads=["ident"], writes=["identf"], dbg="identf")
    s.dma(lambda g: g.dma_start(out=mask_sb[:], in_=maskd[:, :]),
          writes=["mask"], dbg="ld mask")
    s.op("vector", lambda v: v.memset(epsv[:, :], 1e-5), writes=["eps"],
         dbg="eps")
    s.op("vector", lambda v: v.memset(zrow[:, :], 0.0), writes=["zrow"],
         dbg="zrow")

    for i, dst in enumerate([bqb, bk1b, bvb, bk2b, bpb, boutb]):
        s.dma(lambda g, i=i: g.dma_start(
            out=stage[1][:, :],
            in_=brows[i:i + 1, :].partition_broadcast(128)),
            writes=["stage1"], dbg=f"ld bb{i}", dsem="ws1")
        s.op("scalar", lambda sc, dst=dst: sc.copy(out=dst[:, :],
                                                   in_=stage[1][:, :]),
             reads=["stage1"], writes=[f"bb{i}"], dbg=f"cast bb{i}")
    R_BB = [f"bb{i}" for i in range(6)]

    for wi, (nm, dram) in enumerate([("q", w_q), ("k1", w_k1), ("v", w_v),
                                     ("k2", w_k2), ("p", w_p)]):
        for ki in range(8):
            par = ki % 2
            s.dma(lambda g, dram=dram, ki=ki, par=par: g.dma_start(
                out=stage[par][:, :], in_=dram[ki * 128:(ki + 1) * 128, :]),
                writes=[f"stage{par}"], dbg=f"ld w{nm}.{ki}",
                dsem=f"ws{par}")
            s.op("scalar",
                 lambda sc, nm=nm, ki=ki, par=par: sc.copy(
                     out=wbf[nm][:, ki * 1024:(ki + 1) * 1024],
                     in_=stage[par][:, :]),
                 reads=[f"stage{par}"], writes=[f"w{nm}"],
                 dbg=f"cast w{nm}.{ki}")

    for ki in range(8):
        par = ki % 2
        s.dma(lambda g, ki=ki, par=par: g.dma_start(
            out=stage[par][:, 0:64], in_=w_in[ki * 128:(ki + 1) * 128, :]),
            writes=[f"stage{par}"], dbg=f"ld win.{ki}", dsem=f"ws{par}")
        s.op("scalar", lambda sc, ki=ki, par=par: sc.copy(
            out=win_bf[:, ki * 128: ki * 128 + 64], in_=stage[par][:, 0:64]),
            reads=[f"stage{par}"], writes=["win"], dbg=f"cast win.{ki}a")
        s.op("scalar", lambda sc, ki=ki, par=par: sc.copy(
            out=win_bf[:, ki * 128 + 64: ki * 128 + 128],
            in_=stage[par][:, 0:64]),
            reads=[f"stage{par}"], writes=["win"], dbg=f"cast win.{ki}b")
    s.dma(lambda g: g.dma_start(out=stage[0][0:64, :], in_=w_out[:, :]),
          writes=["stage0", "stg0A"], dbg="ld wout a", dsem="ws0")
    s.dma(lambda g: g.dma_start(out=stage[0][64:128, :], in_=w_out[:, :]),
          writes=["stage0"], dbg="ld wout b", dsem="ws2")
    s.op("scalar", lambda sc: sc.copy(out=wout_bf[:, :], in_=stage[0][:, :]),
         reads=["stage0", "stg0A"], writes=["wout"], dbg="cast wout")
    for nm, dram, c0 in (("w1", w1d, 0), ("b1", b1d, 16), ("w2", w2d, 32)):
        s.dma(lambda g, dram=dram, c0=c0: g.dma_start(
            out=ffn_small[0:64, c0:c0 + 16], in_=dram[:, :]),
            writes=["ffn_small"], dbg=f"ld {nm} a")
        s.dma(lambda g, dram=dram, c0=c0: g.dma_start(
            out=ffn_small[64:128, c0:c0 + 16], in_=dram[:, :]),
            writes=["ffn_small"], dbg=f"ld {nm} b")
    s.dma(lambda g: g.dma_start(out=bin_dup[0:64, 0:1],
                                in_=bind[0:1, :].transpose([1, 0])),
          writes=["bin"], dbg="ld bin a")
    s.dma(lambda g: g.dma_start(out=bin_dup[64:128, 0:1],
                                in_=bind[0:1, :].transpose([1, 0])),
          writes=["bin"], dbg="ld bin b")
    s.dma(lambda g: g.dma_start(out=b2h[0:64, 0:1],
                                in_=b2d[0:1, :].transpose([1, 0])),
          writes=["b2h"], dbg="ld b2 a")
    s.dma(lambda g: g.dma_start(out=b2h[64:128, 0:1],
                                in_=b2d[0:1, :].transpose([1, 0])),
          writes=["b2h"], dbg="ld b2 b")
    s.op("vector", lambda v: v.tensor_copy(out=w1p[0:64, :],
                                           in_=ffn_small[0:64, 0:16:2]),
         reads=["ffn_small"], writes=["w1p"], dbg="w1p a")
    s.op("vector", lambda v: v.tensor_copy(out=w1p[64:128, :],
                                           in_=ffn_small[64:128, 1:16:2]),
         reads=["ffn_small"], writes=["w1p"], dbg="w1p b")
    s.op("vector", lambda v: v.tensor_copy(out=w2p[0:64, :],
                                           in_=ffn_small[0:64, 32:48:2]),
         reads=["ffn_small"], writes=["w2p"], dbg="w2p a")
    s.op("vector", lambda v: v.tensor_copy(out=w2p[64:128, :],
                                           in_=ffn_small[64:128, 33:48:2]),
         reads=["ffn_small"], writes=["w2p"], dbg="w2p b")
    s.op("vector", lambda v: v.tensor_copy(out=b1eff[0:64, :],
                                           in_=ffn_small[0:64, 16:32:2]),
         reads=["ffn_small"], writes=["b1eff"], dbg="b1eff a")
    s.op("vector", lambda v: v.tensor_copy(out=b1eff[64:128, :],
                                           in_=ffn_small[64:128, 17:32:2]),
         reads=["ffn_small"], writes=["b1eff"], dbg="b1eff b")
    s.op("vector", lambda v: v.scalar_tensor_tensor(
        out=b1eff[:, :], in0=w1p[:, :], scalar=bin_dup[:, 0:1],
        in1=b1eff[:, :], op0=OP.mult, op1=OP.add),
        reads=["w1p", "bin", "b1eff"], writes=["b1eff"], dbg="b1eff fold")
    s.op("vector", lambda v: v.tensor_scalar(out=b2h[:, :], in0=b2h[:, :],
                                             scalar1=0.5, scalar2=None,
                                             op0=OP.mult),
         reads=["b2h"], writes=["b2h"], dbg="b2 half")

    s.dma(lambda g: g.dma_start(out=x_tok[0][:, :], in_=xin[0, 0:128, :]),
          writes=["x_tok0"], dbg="ld x b0c0", dsem="sx0")

    def transpose_set(src_fn, dst_emit, reads, writes, dbg, bank):
        if bank in (0, 1, 2, 3):
            ptv = PSB_BF[:, bank * 1024:(bank + 1) * 1024]
            breg = R_PS[bank]
        elif bank == "poA":
            ptv = PO_BF[:, 0:1024]
            breg = "P!poA"
        else:
            i = int(bank[-1])
            ptv = PP_BF[i]
            breg = R_PP[i]
        for f in range(8):
            s.op("tensor",
                 lambda t, f=f, ptv=ptv: t.transpose(
                     ptv[:, f * 128:(f + 1) * 128], src_fn(f), ident_sb[:]),
                 reads=list(reads) + ["ident"], writes=[breg],
                 dbg=f"{dbg} T{f}")
        s.op("scalar", lambda sc, ptv=ptv: dst_emit(sc, ptv),
             reads=[breg], writes=list(writes), dbg=f"{dbg} evac")

    PBANKS = [
        (pp[0][:, :], R_PP[0]),
        (pp[1][:, :], R_PP[1]),
        (po[:, 0:512], "P!poA"),
        (po[:, 512:1024], "P!poB"),
    ]

    def projection(wname, out_emit, out_regions, dbg, xpar=0,
                   extra_reads=(), pbank=(0, 1)):
        for half in range(2):
            bview, breg = PBANKS[pbank[half]]
            for ki in range(8):
                s.op("tensor",
                     lambda t, ki=ki, half=half, bview=bview, wname=wname,
                     xpar=xpar: t.matmul(
                         bview,
                         lhsT=xT_bf[xpar][:, ki * 128:(ki + 1) * 128],
                         rhs=wbf[wname][:, ki * 1024 + half * 512:
                                        ki * 1024 + (half + 1) * 512],
                         start=(ki == 0), stop=(ki == 7)),
                     reads=[f"xT{xpar}", f"w{wname}"], writes=[breg],
                     dbg=f"{dbg}.mm{half}.{ki}")
            s.op("vector",
                 lambda v, half=half, bview=bview: out_emit(v, half, bview),
                 reads=[breg] + list(extra_reads),
                 writes=list(out_regions), dbg=f"{dbg}.evac{half}")

    def xprep(u):
        b, c = chunks[u]
        s3 = u % 3
        par = u % 2
        for half in range(2):
            bk = 3 if half == 0 else 2
            for f in range(4):
                fi = half * 4 + f
                s.op("tensor", lambda t, fi=fi, f=f, s3=s3, bk=bk: t.transpose(
                    ps_bank_f32(bk)[:, f * 128:(f + 1) * 128],
                    x_tok[s3][:, fi * 128:(fi + 1) * 128], ident_f32[:]),
                    reads=[f"x_tok{s3}", "identf"], writes=[R_PS[bk]],
                    dbg=f"u{u} xT{fi}")
            s.op("scalar", lambda sc, half=half, par=par, bk=bk: sc.copy(
                out=xT_bf[par][:, half * 512:(half + 1) * 512],
                in_=ps_bank_f32(bk)[:, :]),
                reads=[R_PS[bk]], writes=[f"xT{par}"],
                dbg=f"u{u} xT evac{half}")

    def front(u):
        b, c = chunks[u]
        par = u % 2
        s3 = u % 3
        first = (c == 0)
        XR = f"x_tok{s3}"

        if u + 1 < NN:
            nb_, nc_ = chunks[u + 1]
            ns3 = (u + 1) % 3
            s.dma(lambda g, nb_=nb_, nc_=nc_, ns3=ns3: g.dma_start(
                out=x_tok[ns3][:, :],
                in_=xin[nb_, nc_ * 128:(nc_ + 1) * 128, :]),
                writes=[f"x_tok{ns3}"], dbg=f"ld x u{u+1}",
                dsem=f"sx{ns3}")

        s.op("pool", lambda p, s3=s3: p.tensor_tensor(
            out=x_tok[s3][:, :], in0=x_tok[s3][:, :], in1=bpb[:, :],
            op=OP.add),
            reads=[XR, R_BB[4]], writes=[XR, f"xpb{s3}"], dbg=f"u{u} xpb")

        if first:
            s.op("pool", lambda p, b=b: p.memset(v3(qT_buf[b])[:, :, 0:1], 0.0),
                 reads=[f"qT{b}"], writes=[f"qTc{b}"], dbg=f"b{b} zero qTc")
            s.op("pool", lambda p, b=b: p.memset(v3(k2T_buf[b])[:, :, 0:1], 0.0),
                 reads=[f"k2T{b}"], writes=[f"k2Tc{b}"], dbg=f"b{b} zero k2Tc")
            s.op("pool", lambda p, b=b: p.memset(v3(o1T_buf[b])[:, :, 0:1], 0.0),
                 reads=[f"o1T{b}h0", f"o1T{b}h1"], writes=[f"o1Tc{b}"],
                 dbg=f"b{b} zero o1Tc")
            s.op("vector", lambda v, b=b: v.memset(s1_f32[b][:, :], 0.0),
                 writes=[f"s1f{b}"], dbg=f"b{b} zero s1f")
            s.op("vector", lambda v, b=b: v.memset(s2_f32[b][:, :], 0.0),
                 writes=[f"s2f{b}"], dbg=f"b{b} zero s2f")
            s.op("vector", lambda v, b=b: v.memset(s1_bf[b][:, :], 0.0),
                 writes=[f"s1bf{b}"], dbg=f"b{b} zero s1bf")
            s.op("vector", lambda v, b=b: v.memset(s2_bf[b][:, :], 0.0),
                 writes=[f"s2bf{b}"], dbg=f"b{b} zero s2bf")
        else:
            s.op("pool", lambda p, b=b: p.tensor_copy(
                out=v3(qT_buf[b])[:, :, 0:1],
                in_=v3(qT_buf[b])[:, :, 128:129]),
                reads=[f"qT{b}"], writes=[f"qTc{b}"], dbg=f"u{u} carry qT")
            s.op("pool", lambda p, b=b: p.tensor_copy(
                out=v3(k2T_buf[b])[:, :, 0:1],
                in_=v3(k2T_buf[b])[:, :, 128:129]),
                reads=[f"k2T{b}"], writes=[f"k2Tc{b}"], dbg=f"u{u} carry k2T")
            s.op("pool", lambda p, b=b: p.tensor_copy(
                out=v3(o1T_buf[b])[:, :, 0:1],
                in_=v3(o1T_buf[b])[:, :, 128:129]),
                reads=[f"o1T{b}h0", f"o1T{b}h1"], writes=[f"o1Tc{b}"],
                dbg=f"u{u} carry o1T")

        projection("q",
                   lambda v, half, bank: v.tensor_tensor(
                       out=q_tok[:, half * 512:(half + 1) * 512],
                       in0=bank[:, :],
                       in1=bqb[:, half * 512:(half + 1) * 512], op=OP.add),
                   ["q_tok"], dbg=f"u{u} Q", xpar=par,
                   extra_reads=[R_BB[0]], pbank=(0, 1))
        projection("k2",
                   lambda v, half, bank: v.tensor_tensor(
                       out=ztmp[:, half * 512:(half + 1) * 512],
                       in0=bank[:, :],
                       in1=bk2b[:, half * 512:(half + 1) * 512], op=OP.add),
                   ["stage0"], dbg=f"u{u} z", xpar=par,
                   extra_reads=[R_BB[3]], pbank=(2, 3))
        projection("k1",
                   lambda v, half, bank, par=par: v.tensor_tensor(
                       out=k_tok[par][:, half * 512:(half + 1) * 512],
                       in0=bank[:, :],
                       in1=bk1b[:, half * 512:(half + 1) * 512], op=OP.add),
                   [f"k_tok{par}"], dbg=f"u{u} K", xpar=par,
                   extra_reads=[R_BB[1]], pbank=(0, 1))
        projection("v",
                   lambda v, half, bank, par=par: v.tensor_tensor(
                       out=v_tok[par][:, half * 512:(half + 1) * 512],
                       in0=bank[:, :],
                       in1=bvb[:, half * 512:(half + 1) * 512], op=OP.add),
                   [f"v_tok{par}"], dbg=f"u{u} V", xpar=par,
                   extra_reads=[R_BB[2]], pbank=(2, 3))

        s.op("scalar", lambda sc: sc.activation(
            out=ze[:, :], in_=ztmp[:, :], func=AF.Exp, scale=-0.02 / 32.0),
            reads=["stage0"], writes=["stage1"], dbg=f"u{u} zexp")
        def _zk2(v):
            with nc.allow_low_precision(reason="bf16 operand rounding"):
                return v.tensor_scalar(
                    out=k2_tok[:, :], in0=ze[:, :], scalar1=-0.25,
                    scalar2=0.75, op0=OP.mult, op1=OP.add)
        s.op("vector", _zk2,
             reads=["stage1"], writes=["k2_tok"], dbg=f"u{u} zk2")

        transpose_set(lambda f: q_tok[:, f * 128:(f + 1) * 128],
                      lambda sc, ptv, b=b: sc.copy(
                          out=v3(qT_buf[b])[:, :, 1:129],
                          in_=ptv[:, :].rearrange("p (g c) -> p g c", c=128)),
                      reads=["q_tok"], writes=[f"qT{b}"], dbg=f"u{u} qT",
                      bank="pp0")
        transpose_set(lambda f, par=par: k_tok[par][:, f * 128:(f + 1) * 128],
                      lambda sc, ptv, par=par: sc.copy(out=kT_bf[par][:, :],
                                                       in_=ptv[:, :]),
                      reads=[f"k_tok{par}"], writes=[f"kT{par}"],
                      dbg=f"u{u} kT", bank="pp1")
        transpose_set(lambda f: k2_tok[:, f * 128:(f + 1) * 128],
                      lambda sc, ptv, b=b: sc.copy(
                          out=v3(k2T_buf[b])[:, :, 1:129],
                          in_=ptv[:, :].rearrange("p (g c) -> p g c", c=128)),
                      reads=["k2_tok"], writes=[f"k2T{b}"], dbg=f"u{u} k2T",
                      bank="poA")

    def attn_pass(u, which, mid=None, hooks=None):
        b, c = chunks[u]
        par = u % 2
        first = (c == 0)
        if which == 1:
            lhsT_s = lambda p_, h, par=par: kT_bf[par][64 * h:64 * h + 64,
                                                       p_ * 128:(p_ + 1) * 128]
            rhs_q = lambda p_, h, b=b: qT_buf[b][64 * h:64 * h + 64,
                                                 p_ * 129 + 1: p_ * 129 + 129]
            sc_reads = [f"kT{par}", f"qT{b}"]
            val, val_reads = v_tok[par], [f"v_tok{par}"]
            key_tok, key_reads = k_tok[par], [f"k_tok{par}"]
            sbf, sbf_r = s1_bf[b], f"s1bf{b}"
            sf, sf_r = s1_f32[b], f"s1f{b}"
            pdb, R_PDB = pp[0], R_PP[0]
        else:
            lhsT_s = lambda p_, h, b=b: k2T_buf[b][64 * h:64 * h + 64,
                                                   p_ * 129 + 0: p_ * 129 + 128]
            rhs_q = lambda p_, h: q2_bf[64 * h:64 * h + 64,
                                        p_ * 128:(p_ + 1) * 128]
            sc_reads = [f"k2T{b}", f"k2Tc{b}", "q2"]
            val, val_reads = e_tok, ["e_tokH"]
            key_tok, key_reads = k2s_tok, ["k2s_tok"]
            sbf, sbf_r = s2_bf[b], f"s2bf{b}"
            sf, sf_r = s2_f32[b], f"s2f{b}"
            pdb, R_PDB = pp[1], R_PP[1]

        def scores(p_):
            P2 = p_ % 2
            S4 = p_ % 4
            s.op("tensor", lambda t, p_=p_, P2=P2: t.matmul(
                ps_bank_f32(2 * P2)[:, 0:128], lhsT=lhsT_s(p_, 0),
                rhs=rhs_q(p_, 0), start=True, stop=True),
                reads=sc_reads, writes=[R_PS[2 * P2]],
                dbg=f"u{u}w{which}p{p_} sce")
            s.op("tensor", lambda t, p_=p_, P2=P2: t.matmul(
                ps_bank_f32(2 * P2 + 1)[:, 0:128], lhsT=lhsT_s(p_, 1),
                rhs=rhs_q(p_, 1), start=True, stop=True),
                reads=sc_reads, writes=[R_PS[2 * P2 + 1]],
                dbg=f"u{u}w{which}p{p_} sco")
            s.op("vector", lambda v, p_=p_, P2=P2, S2=p_ % 2: v.tensor_tensor(
                out=smT[S2][:, :].rearrange("p (g c) -> p g c", c=128),
                in0=ps_pair(P2),
                in1=mask_sb[:, :].unsqueeze(1).broadcast_to([128, 2, 128]),
                op=OP.mult),
                reads=[R_PS[2 * P2], R_PS[2 * P2 + 1], "mask"],
                writes=[f"smT{p_ % 2}"], dbg=f"u{u}w{which}p{p_} mask")

        def states():
            for p_ in range(8):
                s.op("tensor", lambda t, p_=p_: t.matmul(
                    pdb[0:64, p_ * 64:(p_ + 1) * 64],
                    lhsT=key_tok[:, p_ * 128: p_ * 128 + 64],
                    rhs=val[:, p_ * 128: p_ * 128 + 64],
                    start=True, stop=True),
                    reads=key_reads + (["e_tok0", "e_tok1"]
                                       if which == 2 else list(val_reads)),
                    writes=[R_PDB],
                    dbg=f"u{u}w{which}p{p_} ste")
                s.op("tensor", lambda t, p_=p_: t.matmul(
                    pdb[64:128, p_ * 64:(p_ + 1) * 64],
                    lhsT=key_tok[:, p_ * 128 + 64: p_ * 128 + 128],
                    rhs=val[:, p_ * 128 + 64: p_ * 128 + 128],
                    start=True, stop=True, tile_position=(0, 64)),
                    reads=key_reads + (["e_tok0", "e_tok1"]
                                       if which == 2 else list(val_reads)),
                    writes=[R_PDB],
                    dbg=f"u{u}w{which}p{p_} sto")

        def dummies():
            for bi in range(2):
                s.op("tensor", lambda t, bi=bi: t.matmul(
                    po[:, bi * 512:(bi + 1) * 512], lhsT=zrow[0:1, 0:128],
                    rhs=zrow[0:1, 128:640], start=True, stop=False,
                    skip_group_check=True),
                    reads=["zrow"],
                    writes=["P!poA" if bi == 0 else "P!poB"],
                    dbg=f"u{u}w{which} ydz{bi}")

        if which == 2:
            scores(0)
            scores(1)
            dummies()
        else:
            dummies()
            scores(0)
            scores(1)
            states()
        if mid is not None:
            mid()
        if which == 2:
            for bi in range(2):
                s.op("tensor", lambda t, bi=bi, b=b: t.matmul(
                    po[:, bi * 512:(bi + 1) * 512], lhsT=ident_sb[:, :],
                    rhs=v3(o1T_buf[b])[:, bi * 4:(bi + 1) * 4, 1:129],
                    start=False, stop=False, skip_group_check=True),
                    reads=[f"o1T{b}h{bi}", "ident"],
                    writes=["P!poA" if bi == 0 else "P!poB"],
                    dbg=f"u{u}w{which} oid{bi}")
        for p_ in range(8):
            S4 = p_ % 4
            cb = po_col(p_)
            rp = R_PO(p_)
            last = (p_ % 4 == 3)
            oie_reads = ([f"e_tok{p_ // 4}"] if which == 2
                         else list(val_reads))
            s.op("tensor", lambda t, p_=p_, S2=p_ % 2, cb=cb: t.matmul(
                po[0:64, cb:cb + 128], lhsT=val[:, p_ * 128: p_ * 128 + 64],
                rhs=smT[S2][:, 0:128], start=False, stop=False,
                skip_group_check=True),
                reads=oie_reads + [f"smT{p_ % 2}"], writes=[rp],
                dbg=f"u{u}w{which}p{p_} oie")
            s.op("tensor", lambda t, p_=p_, S2=p_ % 2, cb=cb: t.matmul(
                po[64:128, cb:cb + 128],
                lhsT=val[:, p_ * 128 + 64: p_ * 128 + 128],
                rhs=smT[S2][:, 128:256], start=False, stop=False,
                tile_position=(0, 64), skip_group_check=True),
                reads=oie_reads + [f"smT{p_ % 2}"], writes=[rp],
                dbg=f"u{u}w{which}p{p_} oio")
            if which == 1:
                rhs_pair = (lambda p_=p_, b=b:
                            qT_buf[b][:, p_ * 129 + 1: p_ * 129 + 129])
            else:
                rhs_pair = (lambda p_=p_:
                            q2_bf[:, p_ * 128:(p_ + 1) * 128])
            s.op("tensor", lambda t, p_=p_, cb=cb, last=last,
                 rhs_pair=rhs_pair: t.matmul(
                po[:, cb:cb + 128], lhsT=sbf[:, p_ * 128:(p_ + 1) * 128],
                rhs=rhs_pair(), start=False, stop=last,
                skip_group_check=True),
                reads=[sbf_r] + sc_reads, writes=[rp],
                dbg=f"u{u}w{which}p{p_} osd")

            if last:
                bb = p_ // 4
                for pe_ in range(p_ - 3, p_ + 1):
                    ce = po_col(pe_)
                    rpe = R_PO(pe_)
                    if which == 1:
                        ee = ("scalar" if (bb == 0 or pe_ % 2 == 0)
                              else "vector")
                        def _emit_o1(x, pe_=pe_, ce=ce, b=b):
                            if hasattr(x, 'copy'):
                                return x.copy(
                                    out=o1T_buf[b][:, pe_ * 129 + 1:
                                                   pe_ * 129 + 129],
                                    in_=po[:, ce:ce + 128])
                            return x.tensor_copy(
                                out=o1T_buf[b][:, pe_ * 129 + 1:
                                               pe_ * 129 + 129],
                                in_=po[:, ce:ce + 128])
                        s.op(ee, _emit_o1,
                             reads=[rpe], writes=[f"o1T{b}h{bb}"],
                             dbg=f"u{u}p{pe_} o1evac")
                    else:
                        ee = ("scalar" if (bb == 0 or pe_ % 2 == 0)
                              else "vector")
                        def _emit_y(x, pe_=pe_, ce=ce):
                            if hasattr(x, 'copy'):
                                return x.copy(
                                    out=yT_bf[:, pe_ * 128:(pe_ + 1) * 128],
                                    in_=po[:, ce:ce + 128])
                            return x.tensor_copy(
                                out=yT_bf[:, pe_ * 128:(pe_ + 1) * 128],
                                in_=po[:, ce:ce + 128])
                        s.op(ee, _emit_y,
                             reads=[rpe], writes=[f"yTh{bb}"],
                             dbg=f"u{u}p{pe_} yevac")
                if which == 1:
                    hh = p_ // 4
                    s.op("pool", lambda p, hh=hh, par=par, b=b:
                         p.tensor_tensor(
                             out=e_bf[:, hh * 512:(hh + 1) * 512].rearrange(
                                 "p (g c) -> p g c", c=128),
                             in0=xT_bf[par][:, hh * 512:(hh + 1) * 512]
                             .rearrange("p (g c) -> p g c", c=128),
                             in1=v3(o1T_buf[b])[:, hh * 4:hh * 4 + 4, 0:128],
                             op=OP.subtract),
                         reads=[f"xT{par}", f"o1T{b}h{hh}", f"o1Tc{b}"],
                         writes=[f"e_bf{hh}"], dbg=f"u{u}h{hh} E")
            if hooks and p_ in hooks:
                hooks[p_]()
            if p_ + 2 < 8:
                scores(p_ + 2)
        if which == 1 and first:
            s.op("pool", lambda p: p.memset(
                e_bf[:, :].rearrange("p (g c) -> p g c", c=128)[:, :, 0:1],
                0.0),
                reads=["e_bf0", "e_bf1"], writes=["e_bf0", "e_bf1"],
                dbg=f"u{u} E z0")
        if which == 2:
            states()
        s.op("vector", lambda v: v.tensor_tensor(
            out=sf[:, :], in0=sf[:, :], in1=pdb[:, :], op=OP.add),
            reads=[sf_r, R_PDB], writes=[sf_r], dbg=f"u{u}w{which} sfold")
        s.op("pool", lambda p: p.tensor_copy(
            out=sbf[0:64, :].rearrange("p (g c) -> p g c", c=128)[:, :, 0:64],
            in_=sf[0:64, :].rearrange("p (g c) -> p g c", c=64)),
            reads=[sf_r], writes=[sbf_r], dbg=f"u{u}w{which} scastA")
        s.op("pool", lambda p: p.tensor_copy(
            out=sbf[64:128, :].rearrange(
                "p (g c) -> p g c", c=128)[:, :, 64:128],
            in_=sf[64:128, :].rearrange("p (g c) -> p g c", c=64)),
            reads=[sf_r], writes=[sbf_r], dbg=f"u{u}w{which} scastB")

    def back1a1(u):
        b, c = chunks[u]
        s.op("scalar", lambda sc, b=b: sc.activation(
            out=q2_bf[:, :].rearrange("p (g c) -> p g c", c=128),
            in_=v3(qT_buf[b])[:, :, 0:128], func=AF.Prelu, alpha=50.0,
            scale=0.0025),
            reads=[f"qT{b}", f"qTc{b}"], writes=["q2"], dbg=f"u{u} q2")
        attn_pass(u, 1)

    def back1a2(u):
        b, c = chunks[u]

        def k2sT_set():
            transpose_set(lambda f, b=b: k2T_buf[b][:, f * 129 + 0:
                                                    f * 129 + 128],
                          lambda sc, ptv: sc.copy(out=k2s_tok[:, :],
                                                  in_=ptv[:, :]),
                          reads=[f"k2T{b}", f"k2Tc{b}"], writes=["k2s_tok"],
                          dbg=f"u{u} k2sT", bank="pp0")

        def eT_half(hh):
            for f in range(4):
                fi = hh * 4 + f
                s.op("tensor", lambda t, fi=fi, hh=hh: t.transpose(
                    PP_BF[1][:, fi * 128:(fi + 1) * 128],
                    e_bf[:, fi * 128:(fi + 1) * 128], ident_sb[:]),
                    reads=[f"e_bf{hh}", "ident"], writes=[R_PP[1]],
                    dbg=f"u{u} eT{fi}")
            s.op("scalar", lambda sc, hh=hh: sc.copy(
                out=e_tok[:, hh * 512:(hh + 1) * 512],
                in_=PP_BF[1][:, hh * 512:(hh + 1) * 512]),
                reads=[R_PP[1]], writes=[f"e_tok{hh}"],
                dbg=f"u{u} eT evac{hh}")
        eT_half(0)
        attn_pass(u, 2, hooks={1: k2sT_set, 2: lambda: eT_half(1)})

    def backWp(u):
        b, c = chunks[u]
        s3 = u % 3
        for half in range(2):
            bank = half
            for g_ in range(8):
                s.op("tensor", lambda t, g_=g_, half=half, bank=bank:
                     t.matmul(
                         pp[bank][:, :],
                         lhsT=yT_bf[:, g_ * 128:(g_ + 1) * 128],
                         rhs=wbf["p"][:, g_ * 1024 + half * 512:
                                      g_ * 1024 + (half + 1) * 512],
                         start=(g_ == 0), stop=(g_ == 7)),
                     reads=[f"yTh{g_ // 4}", "wp"], writes=[R_PP[bank]],
                     dbg=f"u{u} wp.mm{half}.{g_}")
            s.op("vector", lambda v, half=half, bank=bank, s3=s3:
                 v.scalar_tensor_tensor(
                     out=r1[:, half * 512:(half + 1) * 512],
                     in0=pp[bank][:, :], scalar=1.0,
                     in1=x_tok[s3][:, half * 512:(half + 1) * 512],
                     op0=OP.mult, op1=OP.add,
                     accum_out=st1[:, half:half + 1]),
                 reads=[R_PP[bank], f"x_tok{s3}", f"xpb{s3}"],
                 writes=["r1", "st1"], dbg=f"u{u} r1.{half}")

    def ln_chain(u, st, src_reg, src, out_emit, out_regions, dbg):
        streg = "st1" if st is st1 else "st2"
        s.op("scalar", lambda sc: sc.activation(
            out=yT_bf[:, :], in_=src[:, :], func=AF.Square,
            accum_out=st[:, 2:3]),
            reads=[src_reg], writes=["yTh0", "yTh1", streg + "q"],
            dbg=f"{dbg} sq")
        s.op("scalar", lambda sc: sc.activation(
            out=st[:, 3:4], in_=st[:, 0:1], func=AF.Identity,
            bias=st[:, 1:2]),
            reads=[streg], writes=[streg + "t1"], dbg=f"{dbg} t1")
        s.op("scalar", lambda sc: sc.activation(
            out=st[:, 3:4], in_=st[:, 3:4], func=AF.Identity,
            scale=1.0 / D),
            reads=[streg + "t1"], writes=[streg + "m0"], dbg=f"{dbg} mean")
        s.op("scalar", lambda sc: sc.activation(
            out=st[:, 4:5], in_=st[:, 3:4], func=AF.Square),
            reads=[streg + "m0"], writes=[streg + "ms"], dbg=f"{dbg} msq")
        s.op("scalar", lambda sc: sc.activation(
            out=st[:, 5:6], in_=st[:, 4:5], func=AF.Identity,
            scale=-1.0, bias=epsv[:, 0:1]),
            reads=[streg + "ms", "eps"], writes=[streg + "t2"],
            dbg=f"{dbg} t2")
        s.op("scalar", lambda sc: sc.activation(
            out=st[:, 6:7], in_=st[:, 2:3], func=AF.Ln,
            scale=1.0 / D, bias=st[:, 5:6]),
            reads=[streg + "q", streg + "t2"], writes=[streg + "l"],
            dbg=f"{dbg} ln")
        s.op("scalar", lambda sc: sc.activation(
            out=st[:, 7:8], in_=st[:, 6:7], func=AF.Exp, scale=-0.5),
            reads=[streg + "l"], writes=[streg + "r"], dbg=f"{dbg} rstd")
        s.op("scalar", lambda sc: sc.activation(
            out=st[:, 4:5], in_=st[:, 3:4], func=AF.Identity,
            scale=st[:, 7:8]),
            reads=[streg + "m0", streg + "r"], writes=[streg + "n"],
            dbg=f"{dbg} mr")
        out_emit()

    def back1b_act(u):
        ln_chain(u, st1, "r1", r1, lambda: None, [], dbg=f"u{u} ln1")

    def back1b_norm(u):
        par = u % 2
        s.op("vector", lambda v, par=par: v.tensor_scalar(
            out=r1n[par][:, :], in0=r1[:, :],
            scalar1=st1[:, 7:8], scalar2=st1[:, 4:5],
            op0=OP.mult, op1=OP.subtract),
            reads=["r1", "st1r", "st1n"], writes=[f"r1n{par}"],
            dbg=f"u{u} ln1 norm")

    def back1c(u):
        par = u % 2
        for half in range(2):
            for f in range(4):
                fi = half * 4 + f
                s.op("tensor", lambda t, fi=fi, f=f, par=par: t.transpose(
                    pp[0][:, f * 128:(f + 1) * 128],
                    r1n[par][:, fi * 128:(fi + 1) * 128], ident_f32[:]),
                    reads=[f"r1n{par}", "identf"], writes=[R_PP[0]],
                    dbg=f"u{u} r1nT{fi}")
            s.op("scalar", lambda sc, half=half, par=par: sc.copy(
                out=r1nT_bf[par][:, half * 512:(half + 1) * 512],
                in_=pp[0][:, :]),
                reads=[R_PP[0]], writes=[f"r1nT{par}"],
                dbg=f"u{u} r1nT evac{half}")

    def back2a(u):
        par = u % 2
        s.op("pool", lambda p, par=par: p.tensor_tensor(
            out=r1nb[:, :], in0=r1n[par][:, :], in1=boutb[:, :], op=OP.add),
            reads=[f"r1n{par}", R_BB[5]], writes=["r1nb"], dbg=f"u{u} r1nb")
        for ki in range(8):
            s.op("tensor", lambda t, ki=ki, par=par: t.matmul(
                ps_bank_f32(1)[:, 0:128],
                lhsT=win_bf[:, ki * 128:(ki + 1) * 128],
                rhs=r1nT_bf[par][:, ki * 128:(ki + 1) * 128],
                start=(ki == 0), stop=(ki == 7)),
                reads=["win", f"r1nT{par}"], writes=[R_PS[1]],
                dbg=f"u{u} sdup.{ki}")
        for j in range(8):
            s.op("scalar", lambda sc, j=j: sc.activation(
                out=h_all[:, j * 128:(j + 1) * 128],
                in_=ps_bank_f32(1)[:, 0:128], func=AF.Relu,
                scale=w1p[:, j:j + 1], bias=b1eff[:, j:j + 1]),
                reads=[R_PS[1], "w1p", "b1eff"], writes=["h_all"],
                dbg=f"u{u} h{j}")

    def back2a_g(u):
        for j in range(8):
            if j == 0:
                s.op("vector", lambda v: v.tensor_scalar(
                    out=g32[:, :], in0=h_all[:, 0:128], scalar1=w2p[:, 0:1],
                    scalar2=b2h[:, 0:1], op0=OP.mult, op1=OP.add),
                    reads=["h_all", "w2p", "b2h"], writes=["g32"],
                    dbg=f"u{u} g0")
            else:
                s.op("vector", lambda v, j=j: v.scalar_tensor_tensor(
                    out=g32[:, :], in0=h_all[:, j * 128:(j + 1) * 128],
                    scalar=w2p[:, j:j + 1], in1=g32[:, :],
                    op0=OP.mult, op1=OP.add),
                    reads=["h_all", "w2p", "g32"], writes=["g32"],
                    dbg=f"u{u} g{j}")
        s.op("vector", lambda v: v.tensor_copy(out=g_bf[:, :], in_=g32[:, :]),
             reads=["g32"], writes=["g_bf"], dbg=f"u{u} g cast")

    def back2b(u):
        for half in range(2):
            bank = half
            s.op("tensor", lambda t, half=half, bank=bank: t.matmul(
                pp[bank][:, :], lhsT=g_bf[:, :],
                rhs=wout_bf[:, half * 512:(half + 1) * 512],
                start=True, stop=True),
                reads=["g_bf", "wout"], writes=[R_PP[bank]],
                dbg=f"u{u} ffo{half}")
            s.op("vector", lambda v, half=half, bank=bank:
                 v.scalar_tensor_tensor(
                     out=cent[:, half * 512:(half + 1) * 512],
                     in0=pp[bank][:, :], scalar=1.0,
                     in1=r1nb[:, half * 512:(half + 1) * 512],
                     op0=OP.mult, op1=OP.add,
                     accum_out=st2[:, half:half + 1]),
                 reads=[R_PP[bank], "r1nb"], writes=["stage1", "st2"],
                 dbg=f"u{u} r2.{half}")

    def back2c(u):
        b, c = chunks[u]
        ln_chain(u, st2, "stage1", cent,
                 lambda: s.op("vector", lambda v: v.tensor_scalar(
                     out=o_tok[:, :], in0=cent[:, :],
                     scalar1=st2[:, 7:8], scalar2=st2[:, 4:5],
                     op0=OP.mult, op1=OP.subtract),
                     reads=["stage1", "st2r", "st2n"], writes=["o_tok"],
                     dbg=f"u{u} ln2 norm"),
                 ["o_tok"], dbg=f"u{u} ln2")
        s.dma(lambda g, b=b, c=c: g.dma_start(
            out=out[b, c * 128:(c + 1) * 128, :], in_=o_tok[:, :]),
            reads=["o_tok"], dbg=f"st out u{u}", dsem="st0")

    chunks = [(c_ % nb, c_ // nb) for c_ in range(nb * nch)] if nb > 1 else \
        [(0, c_) for c_ in range(nch)]
    NN = len(chunks)

    xprep(0)
    front(0)
    if NN > 1:
        xprep(1)
    for t in range(NN):
        if t >= 2:
            back2a(t - 2)
        if t >= 1:
            back1b_act(t - 1)
        if t + 1 < NN:
            front(t + 1)
        if t >= 1:
            back1b_norm(t - 1)
            back1c(t - 1)
        if t >= 2:
            back2a_g(t - 2)
        back1a1(t)
        back1a2(t)
        backWp(t)
        if t + 2 < NN:
            xprep(t + 2)
        if t >= 2:
            back2b(t - 2)
        if t >= 2:
            back2c(t - 2)
    back1b_act(NN - 1)
    back1b_norm(NN - 1)
    back1c(NN - 1)
    for u in (NN - 2, NN - 1):
        back2a(u)
        back2a_g(u)
        back2b(u)
        back2c(u)

    s.finalize()
    sem_keys = ["tensor", "vector", "scalar", "pool", "sync",
                "su", "sid", "ws0", "ws1", "ws2", "sx0", "sx1", "sx2",
                "st0", "st1"]
    from contextlib import ExitStack as _ES
    with _ES() as sctx:
        sems = {k: sctx.enter_context(nc.semaphore(f"se_{k}"))
                for k in sem_keys}
        with nc.Block() as block:
            s.emit_all(block, sems)
    ctx.close()
    return nc


def make_in_maps(inputs, nb=B_PER_CORE, ncores=8):
    x = np.ascontiguousarray(inputs["L0_attn_x"], dtype=np.float32)
    brows = np.ascontiguousarray(np.stack([
        inputs["L0_attn_bq"], inputs["L0_attn_bk1"], inputs["L0_attn_bv"],
        inputs["L0_attn_bk2"], inputs["L0_attn_bp"], inputs["L0_ffn_bout"],
    ], axis=0).astype(np.float32))
    common = {
        "w_q": np.ascontiguousarray(inputs["L0_attn_Wq"], dtype=np.float32),
        "w_k1": np.ascontiguousarray(inputs["L0_attn_Wk1"], dtype=np.float32),
        "w_v": np.ascontiguousarray(inputs["L0_attn_Wv"], dtype=np.float32),
        "w_k2": np.ascontiguousarray(inputs["L0_attn_Wk2"], dtype=np.float32),
        "w_p": np.ascontiguousarray(inputs["L0_attn_Wp"], dtype=np.float32),
        "w_in": np.ascontiguousarray(inputs["L0_ffn_Win"], dtype=np.float32),
        "w_out": np.ascontiguousarray(inputs["L0_ffn_Wout"], dtype=np.float32),
        "w1d": np.ascontiguousarray(inputs["L0_ffn_W1"], dtype=np.float32),
        "b1d": np.ascontiguousarray(inputs["L0_ffn_b1"], dtype=np.float32),
        "w2d": np.ascontiguousarray(inputs["L0_ffn_W2"], dtype=np.float32),
        "b2d": np.ascontiguousarray(inputs["L0_ffn_b2"],
                                    dtype=np.float32).reshape(1, 64),
        "bind": np.ascontiguousarray(inputs["L0_ffn_bin_"],
                                     dtype=np.float32).reshape(1, 64),
        "brows": brows,
        "identd": np.eye(128, dtype=ml_dtypes.bfloat16),
        "maskd": np.triu(np.ones((128, 128))).astype(ml_dtypes.bfloat16),
    }
    in_maps = []
    for i in range(ncores):
        m = dict(common)
        m["xin"] = np.ascontiguousarray(x[nb * i: nb * (i + 1)])
        in_maps.append(m)
    return in_maps


def kernel(**inputs):
    nc = build_program()
    in_maps = make_in_maps(inputs)
    res = run_bass_kernel_spmd(nc, in_maps, core_ids=list(range(8)))
    return np.concatenate([r["out"] for r in res.results], axis=0)


# version 41
# speedup vs baseline: 1.0112x; 1.0112x over previous
import numpy as np
import ml_dtypes
import concourse.bass as bass
import concourse.mybir as mybir
from concourse.bass_utils import run_bass_kernel_spmd

F32 = mybir.dt.float32
BF16 = mybir.dt.bfloat16
AF = mybir.ActivationFunctionType
OP = mybir.AluOpType

D = 1024
L = 2048
B_PER_CORE = 2
NCH = L // 128

SAME_ENGINE_WINDOW = 4


class _Op:
    __slots__ = ("eng", "emit", "deps", "need_inc", "is_dma", "count_after",
                 "idx", "dbg", "sem_key")

    def __init__(self, eng, emit, idx, is_dma, dbg, sem_key):
        self.eng = eng
        self.emit = emit
        self.idx = idx
        self.is_dma = is_dma
        self.dbg = dbg
        self.sem_key = sem_key
        self.deps = []
        self.need_inc = is_dma
        self.count_after = None


class Sched:

    ENGS = ("tensor", "vector", "scalar", "pool", "sync")

    def __init__(self):
        self.ops = {e: [] for e in self.ENGS}
        self.reg = {}

    def op(self, eng, emit, reads=(), writes=(), is_dma=False, dbg="",
           sem_key=None):
        o = _Op(eng, emit, len(self.ops[eng]), is_dma, dbg,
                sem_key if sem_key is not None else eng)
        deps = set()
        for r in reads:
            st = self.reg.get(r)
            if st is None:
                continue
            if st["w"] is not None:
                deps.add(st["w"])
            if r.startswith("P!"):
                for rd in st["r"]:
                    if rd.eng != eng:
                        deps.add(rd)
        for w in writes:
            st = self.reg.get(w)
            if st:
                if st["w"] is not None:
                    deps.add(st["w"])
                deps.update(st["r"])
        for d in deps:
            if d is o:
                continue
            if d.eng == eng:
                if eng in ("tensor", "sync"):
                    continue
                if o.idx - d.idx > SAME_ENGINE_WINDOW:
                    continue
            o.deps.append(d)
        for r in reads:
            st = self.reg.setdefault(r, {"w": None, "r": []})
            st["r"].append(o)
        for w in writes:
            self.reg[w] = {"w": o, "r": []}
        self.ops[eng].append(o)
        return o

    def dma(self, emit, dbg="", reads=(), writes=(), dsem="su"):
        return self.op("sync", emit, reads=reads, writes=writes,
                       is_dma=True, dbg=dbg, sem_key=dsem)

    def finalize(self):
        for e in self.ENGS:
            for o in self.ops[e]:
                for d in o.deps:
                    d.need_inc = True
        self.totals = {}
        counts = {}
        for e in self.ENGS:
            for o in self.ops[e]:
                if o.need_inc or o.is_dma:
                    counts[o.sem_key] = counts.get(o.sem_key, 0) + (
                        16 if o.is_dma else 1)
                    o.count_after = counts[o.sem_key]
        self.totals = counts
        for e in self.ENGS:
            for o in self.ops[e]:
                if o.is_dma and o.sem_key == "su":
                    o.count_after = counts.get("su", 0)

    def emit_all(self, block, sems):
        outer = self

        def run_stream(eng_name):
            def f(eng_obj):
                waited = {}
                for o in outer.ops[eng_name]:
                    need = {}
                    for d in o.deps:
                        v = d.count_after
                        assert v is not None, (o.dbg, d.dbg)
                        if need.get(d.sem_key, 0) < v:
                            need[d.sem_key] = v
                    for pk in sorted(need):
                        v = need[pk]
                        if waited.get(pk, 0) < v:
                            eng_obj.wait_ge(sems[pk], v)
                            waited[pk] = v
                    inst = o.emit(eng_obj)
                    if o.need_inc:
                        inst.then_inc(sems[o.sem_key],
                                      16 if o.is_dma else 1)
                if eng_name == "sync":
                    for k, v in sorted(outer.totals.items()):
                        if k not in outer.ENGS and v:
                            eng_obj.wait_ge(sems[k], v)
            return f

        block.sync(run_stream("sync"))
        block.tensor(run_stream("tensor"))
        block.vector(run_stream("vector"))
        block.scalar(run_stream("scalar"))
        block.gpsimd(run_stream("pool"))


def build_program(nb=B_PER_CORE, nch=NCH):
    nc = bass.Bass("TRN2", target_bir_lowering=False)
    s = Sched()

    xin = nc.dram_tensor("xin", [nb, L, D], F32, kind="ExternalInput").ap()
    out = nc.dram_tensor("out", [nb, L, D], F32, kind="ExternalOutput").ap()
    w_q = nc.dram_tensor("w_q", [D, D], F32, kind="ExternalInput").ap()
    w_k1 = nc.dram_tensor("w_k1", [D, D], F32, kind="ExternalInput").ap()
    w_v = nc.dram_tensor("w_v", [D, D], F32, kind="ExternalInput").ap()
    w_k2 = nc.dram_tensor("w_k2", [D, D], F32, kind="ExternalInput").ap()
    w_p = nc.dram_tensor("w_p", [D, D], F32, kind="ExternalInput").ap()
    w_in = nc.dram_tensor("w_in", [D, 64], F32, kind="ExternalInput").ap()
    w_out = nc.dram_tensor("w_out", [64, D], F32, kind="ExternalInput").ap()
    w1d = nc.dram_tensor("w1d", [64, 16], F32, kind="ExternalInput").ap()
    b1d = nc.dram_tensor("b1d", [64, 16], F32, kind="ExternalInput").ap()
    w2d = nc.dram_tensor("w2d", [64, 16], F32, kind="ExternalInput").ap()
    b2d = nc.dram_tensor("b2d", [1, 64], F32, kind="ExternalInput").ap()
    bind = nc.dram_tensor("bind", [1, 64], F32, kind="ExternalInput").ap()
    brows = nc.dram_tensor("brows", [6, D], F32, kind="ExternalInput").ap()
    identd = nc.dram_tensor("identd", [128, 128], BF16, kind="ExternalInput").ap()
    maskd = nc.dram_tensor("maskd", [128, 128], BF16, kind="ExternalInput").ap()

    from contextlib import ExitStack
    ctx = ExitStack()
    e = ctx.enter_context

    stage = [e(nc.sbuf_tensor(f"stage{i}", [128, 1024], F32)) for i in range(2)]
    wbf = {}
    for nm in ("q", "k1", "v", "k2", "p"):
        wbf[nm] = e(nc.sbuf_tensor(f"w{nm}_bf", [128, 8 * 1024], BF16))
    win_bf = e(nc.sbuf_tensor("win_bf", [128, 8 * 128], BF16))
    wout_bf = e(nc.sbuf_tensor("wout_bf", [128, 1024], BF16))
    w1p = e(nc.sbuf_tensor("w1p", [128, 8], F32))
    w2p = e(nc.sbuf_tensor("w2p", [128, 8], F32))
    b1eff = e(nc.sbuf_tensor("b1eff", [128, 8], F32))
    ffn_small = e(nc.sbuf_tensor("ffn_small", [128, 48], F32))
    bin_dup = e(nc.sbuf_tensor("bin_dup", [128, 1], F32))
    b2h = e(nc.sbuf_tensor("b2h", [128, 1], F32))
    epsv = e(nc.sbuf_tensor("epsv", [128, 1], F32))
    ident_sb = e(nc.sbuf_tensor("ident_sb", [128, 128], BF16))
    ident_f32 = e(nc.sbuf_tensor("ident_f32", [128, 128], F32))
    mask_sb = e(nc.sbuf_tensor("mask_sb", [128, 128], BF16))
    bqb = e(nc.sbuf_tensor("bqb", [128, D], BF16))
    bk1b = e(nc.sbuf_tensor("bk1b", [128, D], BF16))
    bvb = e(nc.sbuf_tensor("bvb", [128, D], BF16))
    bk2b = e(nc.sbuf_tensor("bk2b", [128, D], BF16))
    bpb = e(nc.sbuf_tensor("bpb", [128, D], BF16))
    boutb = e(nc.sbuf_tensor("boutb", [128, D], BF16))

    x_tok = [e(nc.sbuf_tensor(f"x_tok{i}", [128, D], F32)) for i in range(3)]
    o_tok = e(nc.sbuf_tensor("o_tok", [128, D], F32))
    xT_bf = [e(nc.sbuf_tensor(f"xT_bf{i}", [128, D], BF16)) for i in range(2)]
    q_tok = e(nc.sbuf_tensor("q_tok", [128, D], BF16))
    k_tok = [e(nc.sbuf_tensor(f"k_tok{i}", [128, D], BF16)) for i in range(2)]
    v_tok = [e(nc.sbuf_tensor(f"v_tok{i}", [128, D], BF16)) for i in range(2)]
    k2_tok = e(nc.sbuf_tensor("k2_tok", [128, D], BF16))
    ztmp = stage[0]
    ze = stage[1]
    kT_bf = [e(nc.sbuf_tensor(f"kT_bf{i}", [128, D], BF16)) for i in range(2)]
    qT_buf = [e(nc.sbuf_tensor(f"qT_buf{i}", [128, 8 * 129], BF16))
              for i in range(nb)]
    k2T_buf = [e(nc.sbuf_tensor(f"k2T_buf{i}", [128, 8 * 129], BF16))
               for i in range(nb)]
    o1T_buf = [e(nc.sbuf_tensor(f"o1T_buf{i}", [128, 8 * 129], BF16))
               for i in range(nb)]
    q2_bf = e(nc.sbuf_tensor("q2_bf", [128, D], BF16))
    e_bf = e(nc.sbuf_tensor("e_bf", [128, D], BF16))
    e_tok = e(nc.sbuf_tensor("e_tok", [128, D], BF16))
    k2s_tok = e(nc.sbuf_tensor("k2s_tok", [128, D], BF16))
    smT = [e(nc.sbuf_tensor(f"smT{i}", [128, 256], BF16)) for i in range(4)]
    yT_bf = e(nc.sbuf_tensor("yT_bf", [128, D], BF16))
    s1_f32 = [e(nc.sbuf_tensor(f"s1_f32{i}", [128, 512], F32))
              for i in range(nb)]
    s2_f32 = [e(nc.sbuf_tensor(f"s2_f32{i}", [128, 512], F32))
              for i in range(nb)]
    s1_bf = [e(nc.sbuf_tensor(f"s1_bf{i}", [128, 1024], BF16))
             for i in range(nb)]
    s2_bf = [e(nc.sbuf_tensor(f"s2_bf{i}", [128, 1024], BF16))
             for i in range(nb)]
    r1 = e(nc.sbuf_tensor("r1", [128, D], F32))
    cent = stage[1]
    r1n = [e(nc.sbuf_tensor(f"r1n{i}", [128, D], F32)) for i in range(2)]
    r1nb = e(nc.sbuf_tensor("r1nb", [128, D], F32))
    r1nT_bf = [e(nc.sbuf_tensor(f"r1nT_bf{i}", [128, D], BF16))
               for i in range(2)]
    h_all = e(nc.sbuf_tensor("h_all", [128, 1024], BF16))
    g32 = e(nc.sbuf_tensor("g32", [128, 128], F32))
    g_bf = e(nc.sbuf_tensor("g_bf", [128, 128], BF16))
    st1 = e(nc.sbuf_tensor("st1", [128, 8], F32))
    st2 = e(nc.sbuf_tensor("st2", [128, 8], F32))
    zrow = e(nc.sbuf_tensor("zrow", [1, 640], BF16))

    pp = [e(nc.psum_tensor(f"pp{i}", [128, 512], F32)) for i in range(2)]
    psb = e(nc.psum_tensor("psb", [128, 2048], F32))
    po = e(nc.psum_tensor("po", [128, 1024], F32))

    R_PP = ["P!pp0", "P!pp1"]
    R_PS = ["P!ps0", "P!ps1", "P!ps2", "P!ps3"]

    PSB_BF = psb[:, :].bitcast(BF16)
    PP_BF = [pp[i][:, :].bitcast(BF16) for i in range(2)]
    PO_BF = po[:, :].bitcast(BF16)

    def ps_bank_f32(bank):
        return psb[:, bank * 512:(bank + 1) * 512]

    def v3(t, c=129):
        return t[:, :].rearrange("p (g c) -> p g c", c=c)

    def ps_pair(par2):
        return psb[:, par2 * 1024:(par2 + 1) * 1024].rearrange(
            "p (g c) -> p g c", c=512)[:, :, 0:128]

    def po_col(p_):
        return (p_ // 4) * 512 + (p_ % 4) * 128

    def R_PO(p_):
        return "P!poA" if p_ < 4 else "P!poB"

    s.dma(lambda g: g.dma_start(out=ident_sb[:], in_=identd[:, :]),
          writes=["ident"], dbg="ld ident", dsem="sid")
    s.op("scalar", lambda sc: sc.copy(out=ident_f32[:, :], in_=ident_sb[:, :]),
         reads=["ident"], writes=["identf"], dbg="identf")
    s.dma(lambda g: g.dma_start(out=mask_sb[:], in_=maskd[:, :]),
          writes=["mask"], dbg="ld mask")
    s.op("vector", lambda v: v.memset(epsv[:, :], 1e-5), writes=["eps"],
         dbg="eps")
    s.op("vector", lambda v: v.memset(zrow[:, :], 0.0), writes=["zrow"],
         dbg="zrow")

    for i, dst in enumerate([bqb, bk1b, bvb, bk2b, bpb, boutb]):
        s.dma(lambda g, i=i: g.dma_start(
            out=stage[1][:, :],
            in_=brows[i:i + 1, :].partition_broadcast(128)),
            writes=["stage1"], dbg=f"ld bb{i}", dsem="ws1")
        s.op("scalar", lambda sc, dst=dst: sc.copy(out=dst[:, :],
                                                   in_=stage[1][:, :]),
             reads=["stage1"], writes=[f"bb{i}"], dbg=f"cast bb{i}")
    R_BB = [f"bb{i}" for i in range(6)]

    for wi, (nm, dram) in enumerate([("q", w_q), ("k1", w_k1), ("v", w_v),
                                     ("k2", w_k2), ("p", w_p)]):
        for ki in range(8):
            par = ki % 2
            s.dma(lambda g, dram=dram, ki=ki, par=par: g.dma_start(
                out=stage[par][:, :], in_=dram[ki * 128:(ki + 1) * 128, :]),
                writes=[f"stage{par}"], dbg=f"ld w{nm}.{ki}",
                dsem=f"ws{par}")
            s.op("scalar",
                 lambda sc, nm=nm, ki=ki, par=par: sc.copy(
                     out=wbf[nm][:, ki * 1024:(ki + 1) * 1024],
                     in_=stage[par][:, :]),
                 reads=[f"stage{par}"], writes=[f"w{nm}"],
                 dbg=f"cast w{nm}.{ki}")

    for ki in range(8):
        par = ki % 2
        s.dma(lambda g, ki=ki, par=par: g.dma_start(
            out=stage[par][:, 0:64], in_=w_in[ki * 128:(ki + 1) * 128, :]),
            writes=[f"stage{par}"], dbg=f"ld win.{ki}", dsem=f"ws{par}")
        s.op("scalar", lambda sc, ki=ki, par=par: sc.copy(
            out=win_bf[:, ki * 128: ki * 128 + 64], in_=stage[par][:, 0:64]),
            reads=[f"stage{par}"], writes=["win"], dbg=f"cast win.{ki}a")
        s.op("scalar", lambda sc, ki=ki, par=par: sc.copy(
            out=win_bf[:, ki * 128 + 64: ki * 128 + 128],
            in_=stage[par][:, 0:64]),
            reads=[f"stage{par}"], writes=["win"], dbg=f"cast win.{ki}b")
    s.dma(lambda g: g.dma_start(out=stage[0][0:64, :], in_=w_out[:, :]),
          writes=["stage0", "stg0A"], dbg="ld wout a", dsem="ws0")
    s.dma(lambda g: g.dma_start(out=stage[0][64:128, :], in_=w_out[:, :]),
          writes=["stage0"], dbg="ld wout b", dsem="ws2")
    s.op("scalar", lambda sc: sc.copy(out=wout_bf[:, :], in_=stage[0][:, :]),
         reads=["stage0", "stg0A"], writes=["wout"], dbg="cast wout")
    for nm, dram, c0 in (("w1", w1d, 0), ("b1", b1d, 16), ("w2", w2d, 32)):
        s.dma(lambda g, dram=dram, c0=c0: g.dma_start(
            out=ffn_small[0:64, c0:c0 + 16], in_=dram[:, :]),
            writes=["ffn_small"], dbg=f"ld {nm} a")
        s.dma(lambda g, dram=dram, c0=c0: g.dma_start(
            out=ffn_small[64:128, c0:c0 + 16], in_=dram[:, :]),
            writes=["ffn_small"], dbg=f"ld {nm} b")
    s.dma(lambda g: g.dma_start(out=bin_dup[0:64, 0:1],
                                in_=bind[0:1, :].transpose([1, 0])),
          writes=["bin"], dbg="ld bin a")
    s.dma(lambda g: g.dma_start(out=bin_dup[64:128, 0:1],
                                in_=bind[0:1, :].transpose([1, 0])),
          writes=["bin"], dbg="ld bin b")
    s.dma(lambda g: g.dma_start(out=b2h[0:64, 0:1],
                                in_=b2d[0:1, :].transpose([1, 0])),
          writes=["b2h"], dbg="ld b2 a")
    s.dma(lambda g: g.dma_start(out=b2h[64:128, 0:1],
                                in_=b2d[0:1, :].transpose([1, 0])),
          writes=["b2h"], dbg="ld b2 b")
    s.op("vector", lambda v: v.tensor_copy(out=w1p[0:64, :],
                                           in_=ffn_small[0:64, 0:16:2]),
         reads=["ffn_small"], writes=["w1p"], dbg="w1p a")
    s.op("vector", lambda v: v.tensor_copy(out=w1p[64:128, :],
                                           in_=ffn_small[64:128, 1:16:2]),
         reads=["ffn_small"], writes=["w1p"], dbg="w1p b")
    s.op("vector", lambda v: v.tensor_copy(out=w2p[0:64, :],
                                           in_=ffn_small[0:64, 32:48:2]),
         reads=["ffn_small"], writes=["w2p"], dbg="w2p a")
    s.op("vector", lambda v: v.tensor_copy(out=w2p[64:128, :],
                                           in_=ffn_small[64:128, 33:48:2]),
         reads=["ffn_small"], writes=["w2p"], dbg="w2p b")
    s.op("vector", lambda v: v.tensor_copy(out=b1eff[0:64, :],
                                           in_=ffn_small[0:64, 16:32:2]),
         reads=["ffn_small"], writes=["b1eff"], dbg="b1eff a")
    s.op("vector", lambda v: v.tensor_copy(out=b1eff[64:128, :],
                                           in_=ffn_small[64:128, 17:32:2]),
         reads=["ffn_small"], writes=["b1eff"], dbg="b1eff b")
    s.op("vector", lambda v: v.scalar_tensor_tensor(
        out=b1eff[:, :], in0=w1p[:, :], scalar=bin_dup[:, 0:1],
        in1=b1eff[:, :], op0=OP.mult, op1=OP.add),
        reads=["w1p", "bin", "b1eff"], writes=["b1eff"], dbg="b1eff fold")
    s.op("vector", lambda v: v.tensor_scalar(out=b2h[:, :], in0=b2h[:, :],
                                             scalar1=0.5, scalar2=None,
                                             op0=OP.mult),
         reads=["b2h"], writes=["b2h"], dbg="b2 half")

    s.dma(lambda g: g.dma_start(out=x_tok[0][:, :], in_=xin[0, 0:128, :]),
          writes=["x_tok0"], dbg="ld x b0c0", dsem="sx0")

    def transpose_set(src_fn, dst_emit, reads, writes, dbg, bank):
        if bank in (0, 1, 2, 3):
            ptv = PSB_BF[:, bank * 1024:(bank + 1) * 1024]
            breg = R_PS[bank]
        elif bank == "poA":
            ptv = PO_BF[:, 0:1024]
            breg = "P!poA"
        else:
            i = int(bank[-1])
            ptv = PP_BF[i]
            breg = R_PP[i]
        for f in range(8):
            s.op("tensor",
                 lambda t, f=f, ptv=ptv: t.transpose(
                     ptv[:, f * 128:(f + 1) * 128], src_fn(f), ident_sb[:]),
                 reads=list(reads) + ["ident"], writes=[breg],
                 dbg=f"{dbg} T{f}")
        s.op("scalar", lambda sc, ptv=ptv: dst_emit(sc, ptv),
             reads=[breg], writes=list(writes), dbg=f"{dbg} evac")

    PBANKS = [
        (pp[0][:, :], R_PP[0]),
        (pp[1][:, :], R_PP[1]),
        (po[:, 0:512], "P!poA"),
        (po[:, 512:1024], "P!poB"),
    ]

    def projection(wname, out_emit, out_regions, dbg, xpar=0,
                   extra_reads=(), pbank=(0, 1)):
        for half in range(2):
            bview, breg = PBANKS[pbank[half]]
            for ki in range(8):
                s.op("tensor",
                     lambda t, ki=ki, half=half, bview=bview, wname=wname,
                     xpar=xpar: t.matmul(
                         bview,
                         lhsT=xT_bf[xpar][:, ki * 128:(ki + 1) * 128],
                         rhs=wbf[wname][:, ki * 1024 + half * 512:
                                        ki * 1024 + (half + 1) * 512],
                         start=(ki == 0), stop=(ki == 7)),
                     reads=[f"xT{xpar}", f"w{wname}"], writes=[breg],
                     dbg=f"{dbg}.mm{half}.{ki}")
            s.op("vector",
                 lambda v, half=half, bview=bview: out_emit(v, half, bview),
                 reads=[breg] + list(extra_reads),
                 writes=list(out_regions), dbg=f"{dbg}.evac{half}")

    def xprep(u):
        b, c = chunks[u]
        s3 = u % 3
        par = u % 2
        for half in range(2):
            bk = 3 if half == 0 else 2
            for f in range(4):
                fi = half * 4 + f
                s.op("tensor", lambda t, fi=fi, f=f, s3=s3, bk=bk: t.transpose(
                    ps_bank_f32(bk)[:, f * 128:(f + 1) * 128],
                    x_tok[s3][:, fi * 128:(fi + 1) * 128], ident_f32[:]),
                    reads=[f"x_tok{s3}", "identf"], writes=[R_PS[bk]],
                    dbg=f"u{u} xT{fi}")
            s.op("scalar", lambda sc, half=half, par=par, bk=bk: sc.copy(
                out=xT_bf[par][:, half * 512:(half + 1) * 512],
                in_=ps_bank_f32(bk)[:, :]),
                reads=[R_PS[bk]], writes=[f"xT{par}"],
                dbg=f"u{u} xT evac{half}")

    def front(u):
        b, c = chunks[u]
        par = u % 2
        s3 = u % 3
        first = (c == 0)
        XR = f"x_tok{s3}"

        if u + 1 < NN:
            nb_, nc_ = chunks[u + 1]
            ns3 = (u + 1) % 3
            s.dma(lambda g, nb_=nb_, nc_=nc_, ns3=ns3: g.dma_start(
                out=x_tok[ns3][:, :],
                in_=xin[nb_, nc_ * 128:(nc_ + 1) * 128, :]),
                writes=[f"x_tok{ns3}"], dbg=f"ld x u{u+1}",
                dsem=f"sx{ns3}")

        s.op("pool", lambda p, s3=s3: p.tensor_tensor(
            out=x_tok[s3][:, :], in0=x_tok[s3][:, :], in1=bpb[:, :],
            op=OP.add),
            reads=[XR, R_BB[4]], writes=[XR, f"xpb{s3}"], dbg=f"u{u} xpb")

        if first:
            s.op("pool", lambda p, b=b: p.memset(v3(qT_buf[b])[:, :, 0:1], 0.0),
                 reads=[f"qT{b}"], writes=[f"qTc{b}"], dbg=f"b{b} zero qTc")
            s.op("pool", lambda p, b=b: p.memset(v3(k2T_buf[b])[:, :, 0:1], 0.0),
                 reads=[f"k2T{b}"], writes=[f"k2Tc{b}"], dbg=f"b{b} zero k2Tc")
            s.op("pool", lambda p, b=b: p.memset(v3(o1T_buf[b])[:, :, 0:1], 0.0),
                 reads=[f"o1T{b}h0", f"o1T{b}h1"], writes=[f"o1Tc{b}"],
                 dbg=f"b{b} zero o1Tc")
            s.op("vector", lambda v, b=b: v.memset(s1_f32[b][:, :], 0.0),
                 writes=[f"s1f{b}"], dbg=f"b{b} zero s1f")
            s.op("vector", lambda v, b=b: v.memset(s2_f32[b][:, :], 0.0),
                 writes=[f"s2f{b}"], dbg=f"b{b} zero s2f")
            s.op("vector", lambda v, b=b: v.memset(s1_bf[b][:, :], 0.0),
                 writes=[f"s1bf{b}"], dbg=f"b{b} zero s1bf")
            s.op("vector", lambda v, b=b: v.memset(s2_bf[b][:, :], 0.0),
                 writes=[f"s2bf{b}"], dbg=f"b{b} zero s2bf")
        else:
            s.op("pool", lambda p, b=b: p.tensor_copy(
                out=v3(qT_buf[b])[:, :, 0:1],
                in_=v3(qT_buf[b])[:, :, 128:129]),
                reads=[f"qT{b}"], writes=[f"qTc{b}"], dbg=f"u{u} carry qT")
            s.op("pool", lambda p, b=b: p.tensor_copy(
                out=v3(k2T_buf[b])[:, :, 0:1],
                in_=v3(k2T_buf[b])[:, :, 128:129]),
                reads=[f"k2T{b}"], writes=[f"k2Tc{b}"], dbg=f"u{u} carry k2T")
            s.op("pool", lambda p, b=b: p.tensor_copy(
                out=v3(o1T_buf[b])[:, :, 0:1],
                in_=v3(o1T_buf[b])[:, :, 128:129]),
                reads=[f"o1T{b}h0", f"o1T{b}h1"], writes=[f"o1Tc{b}"],
                dbg=f"u{u} carry o1T")

        projection("q",
                   lambda v, half, bank: v.tensor_tensor(
                       out=q_tok[:, half * 512:(half + 1) * 512],
                       in0=bank[:, :],
                       in1=bqb[:, half * 512:(half + 1) * 512], op=OP.add),
                   ["q_tok"], dbg=f"u{u} Q", xpar=par,
                   extra_reads=[R_BB[0]], pbank=(0, 1))
        projection("k2",
                   lambda v, half, bank: v.tensor_tensor(
                       out=ztmp[:, half * 512:(half + 1) * 512],
                       in0=bank[:, :],
                       in1=bk2b[:, half * 512:(half + 1) * 512], op=OP.add),
                   ["stage0"], dbg=f"u{u} z", xpar=par,
                   extra_reads=[R_BB[3]], pbank=(2, 3))
        projection("k1",
                   lambda v, half, bank, par=par: v.tensor_tensor(
                       out=k_tok[par][:, half * 512:(half + 1) * 512],
                       in0=bank[:, :],
                       in1=bk1b[:, half * 512:(half + 1) * 512], op=OP.add),
                   [f"k_tok{par}"], dbg=f"u{u} K", xpar=par,
                   extra_reads=[R_BB[1]], pbank=(0, 1))
        projection("v",
                   lambda v, half, bank, par=par: v.tensor_tensor(
                       out=v_tok[par][:, half * 512:(half + 1) * 512],
                       in0=bank[:, :],
                       in1=bvb[:, half * 512:(half + 1) * 512], op=OP.add),
                   [f"v_tok{par}"], dbg=f"u{u} V", xpar=par,
                   extra_reads=[R_BB[2]], pbank=(2, 3))

        s.op("scalar", lambda sc: sc.activation(
            out=ze[:, :], in_=ztmp[:, :], func=AF.Exp, scale=-0.02 / 32.0),
            reads=["stage0"], writes=["stage1"], dbg=f"u{u} zexp")
        def _zk2(v):
            with nc.allow_low_precision(reason="bf16 operand rounding"):
                return v.tensor_scalar(
                    out=k2_tok[:, :], in0=ze[:, :], scalar1=-0.25,
                    scalar2=0.75, op0=OP.mult, op1=OP.add)
        s.op("vector", _zk2,
             reads=["stage1"], writes=["k2_tok"], dbg=f"u{u} zk2")

        transpose_set(lambda f: q_tok[:, f * 128:(f + 1) * 128],
                      lambda sc, ptv, b=b: sc.copy(
                          out=v3(qT_buf[b])[:, :, 1:129],
                          in_=ptv[:, :].rearrange("p (g c) -> p g c", c=128)),
                      reads=["q_tok"], writes=[f"qT{b}"], dbg=f"u{u} qT",
                      bank="pp0")
        transpose_set(lambda f, par=par: k_tok[par][:, f * 128:(f + 1) * 128],
                      lambda sc, ptv, par=par: sc.copy(out=kT_bf[par][:, :],
                                                       in_=ptv[:, :]),
                      reads=[f"k_tok{par}"], writes=[f"kT{par}"],
                      dbg=f"u{u} kT", bank="pp1")
        transpose_set(lambda f: k2_tok[:, f * 128:(f + 1) * 128],
                      lambda sc, ptv, b=b: sc.copy(
                          out=v3(k2T_buf[b])[:, :, 1:129],
                          in_=ptv[:, :].rearrange("p (g c) -> p g c", c=128)),
                      reads=["k2_tok"], writes=[f"k2T{b}"], dbg=f"u{u} k2T",
                      bank="poA")

    def attn_pass(u, which, mid=None):
        b, c = chunks[u]
        par = u % 2
        first = (c == 0)
        if which == 1:
            lhsT_s = lambda p_, h, par=par: kT_bf[par][64 * h:64 * h + 64,
                                                       p_ * 128:(p_ + 1) * 128]
            rhs_q = lambda p_, h, b=b: qT_buf[b][64 * h:64 * h + 64,
                                                 p_ * 129 + 1: p_ * 129 + 129]
            sc_reads = [f"kT{par}", f"qT{b}"]
            val, val_reads = v_tok[par], [f"v_tok{par}"]
            key_tok, key_reads = k_tok[par], [f"k_tok{par}"]
            sbf, sbf_r = s1_bf[b], f"s1bf{b}"
            sf, sf_r = s1_f32[b], f"s1f{b}"
            pdb, R_PDB = pp[0], R_PP[0]
        else:
            lhsT_s = lambda p_, h, b=b: k2T_buf[b][64 * h:64 * h + 64,
                                                   p_ * 129 + 0: p_ * 129 + 128]
            rhs_q = lambda p_, h: q2_bf[64 * h:64 * h + 64,
                                        p_ * 128:(p_ + 1) * 128]
            sc_reads = [f"k2T{b}", f"k2Tc{b}", "q2"]
            val, val_reads = e_tok, ["e_tokH"]
            key_tok, key_reads = k2s_tok, ["k2s_tok"]
            sbf, sbf_r = s2_bf[b], f"s2bf{b}"
            sf, sf_r = s2_f32[b], f"s2f{b}"
            pdb, R_PDB = pp[1], R_PP[1]

        def scores(p_):
            P2 = (p_ + (0 if which == 1 else 1)) % 2
            S4 = p_ % 4
            s.op("tensor", lambda t, p_=p_, P2=P2: t.matmul(
                ps_bank_f32(2 * P2)[:, 0:128], lhsT=lhsT_s(p_, 0),
                rhs=rhs_q(p_, 0), start=True, stop=True),
                reads=sc_reads, writes=[R_PS[2 * P2]],
                dbg=f"u{u}w{which}p{p_} sce")
            s.op("tensor", lambda t, p_=p_, P2=P2: t.matmul(
                ps_bank_f32(2 * P2 + 1)[:, 0:128], lhsT=lhsT_s(p_, 1),
                rhs=rhs_q(p_, 1), start=True, stop=True),
                reads=sc_reads, writes=[R_PS[2 * P2 + 1]],
                dbg=f"u{u}w{which}p{p_} sco")
            s.op("vector", lambda v, p_=p_, P2=P2, S4=p_ % 4: v.tensor_tensor(
                out=smT[S4][:, :].rearrange("p (g c) -> p g c", c=128),
                in0=ps_pair(P2),
                in1=mask_sb[:, :].unsqueeze(1).broadcast_to([128, 2, 128]),
                op=OP.mult),
                reads=[R_PS[2 * P2], R_PS[2 * P2 + 1], "mask"],
                writes=[f"smT{p_ % 4}"], dbg=f"u{u}w{which}p{p_} mask")

        def states():
            for p_ in range(8):
                s.op("tensor", lambda t, p_=p_: t.matmul(
                    pdb[0:64, p_ * 64:(p_ + 1) * 64],
                    lhsT=key_tok[:, p_ * 128: p_ * 128 + 64],
                    rhs=val[:, p_ * 128: p_ * 128 + 64],
                    start=True, stop=True),
                    reads=key_reads + (["e_tok0", "e_tok1"]
                                       if which == 2 else list(val_reads)),
                    writes=[R_PDB],
                    dbg=f"u{u}w{which}p{p_} ste")
                s.op("tensor", lambda t, p_=p_: t.matmul(
                    pdb[64:128, p_ * 64:(p_ + 1) * 64],
                    lhsT=key_tok[:, p_ * 128 + 64: p_ * 128 + 128],
                    rhs=val[:, p_ * 128 + 64: p_ * 128 + 128],
                    start=True, stop=True, tile_position=(0, 64)),
                    reads=key_reads + (["e_tok0", "e_tok1"]
                                       if which == 2 else list(val_reads)),
                    writes=[R_PDB],
                    dbg=f"u{u}w{which}p{p_} sto")

        def dummies():
            for bi in range(2):
                s.op("tensor", lambda t, bi=bi: t.matmul(
                    po[:, bi * 512:(bi + 1) * 512], lhsT=zrow[0:1, 0:128],
                    rhs=zrow[0:1, 128:640], start=True, stop=False,
                    skip_group_check=True),
                    reads=["zrow"],
                    writes=["P!poA" if bi == 0 else "P!poB"],
                    dbg=f"u{u}w{which} ydz{bi}")

        dummies()
        scores(0)
        scores(1)
        if which == 1:
            states()
        if mid is not None:
            mid()
        if which == 2:
            for bi in range(2):
                s.op("tensor", lambda t, bi=bi, b=b: t.matmul(
                    po[:, bi * 512:(bi + 1) * 512], lhsT=ident_sb[:, :],
                    rhs=v3(o1T_buf[b])[:, bi * 4:(bi + 1) * 4, 1:129],
                    start=False, stop=False, skip_group_check=True),
                    reads=[f"o1T{b}h{bi}", "ident"],
                    writes=["P!poA" if bi == 0 else "P!poB"],
                    dbg=f"u{u}w{which} oid{bi}")
        def outs(p_):
            cb = po_col(p_)
            rp = R_PO(p_)
            last = (p_ % 4 == 3)
            oie_reads = ([f"e_tok{p_ // 4}"] if which == 2
                         else list(val_reads))
            s.op("tensor", lambda t, p_=p_, S4=p_ % 4, cb=cb: t.matmul(
                po[0:64, cb:cb + 128], lhsT=val[:, p_ * 128: p_ * 128 + 64],
                rhs=smT[S4][:, 0:128], start=False, stop=False,
                skip_group_check=True),
                reads=oie_reads + [f"smT{p_ % 4}"], writes=[rp],
                dbg=f"u{u}w{which}p{p_} oie")
            s.op("tensor", lambda t, p_=p_, S4=p_ % 4, cb=cb: t.matmul(
                po[64:128, cb:cb + 128],
                lhsT=val[:, p_ * 128 + 64: p_ * 128 + 128],
                rhs=smT[S4][:, 128:256], start=False, stop=False,
                tile_position=(0, 64), skip_group_check=True),
                reads=oie_reads + [f"smT{p_ % 4}"], writes=[rp],
                dbg=f"u{u}w{which}p{p_} oio")
            if which == 1:
                rhs_pair = (lambda p_=p_, b=b:
                            qT_buf[b][:, p_ * 129 + 1: p_ * 129 + 129])
            else:
                rhs_pair = (lambda p_=p_:
                            q2_bf[:, p_ * 128:(p_ + 1) * 128])
            s.op("tensor", lambda t, p_=p_, cb=cb, last=last,
                 rhs_pair=rhs_pair: t.matmul(
                po[:, cb:cb + 128], lhsT=sbf[:, p_ * 128:(p_ + 1) * 128],
                rhs=rhs_pair(), start=False, stop=last,
                skip_group_check=True),
                reads=[sbf_r] + sc_reads, writes=[rp],
                dbg=f"u{u}w{which}p{p_} osd")
            if last:
                bb = p_ // 4
                rpe = R_PO(p_)
                for pe_ in range(p_ - 3, p_ + 1):
                    ce = po_col(pe_)
                    if which == 1:
                        ee = ("scalar" if (bb == 0 or pe_ % 2 == 0)
                              else "vector")
                        def _emit_o1(x, pe_=pe_, ce=ce, b=b):
                            if hasattr(x, 'copy'):
                                return x.copy(
                                    out=o1T_buf[b][:, pe_ * 129 + 1:
                                                   pe_ * 129 + 129],
                                    in_=po[:, ce:ce + 128])
                            return x.tensor_copy(
                                out=o1T_buf[b][:, pe_ * 129 + 1:
                                               pe_ * 129 + 129],
                                in_=po[:, ce:ce + 128])
                        s.op(ee, _emit_o1,
                             reads=[rpe], writes=[f"o1T{b}h{bb}"],
                             dbg=f"u{u}p{pe_} o1evac")
                    else:
                        ee = ("scalar" if (bb == 0 or pe_ % 2 == 0)
                              else "vector")
                        def _emit_y(x, pe_=pe_, ce=ce):
                            if hasattr(x, 'copy'):
                                return x.copy(
                                    out=yT_bf[:, pe_ * 128:(pe_ + 1) * 128],
                                    in_=po[:, ce:ce + 128])
                            return x.tensor_copy(
                                out=yT_bf[:, pe_ * 128:(pe_ + 1) * 128],
                                in_=po[:, ce:ce + 128])
                        s.op(ee, _emit_y,
                             reads=[rpe], writes=[f"yTh{bb}"],
                             dbg=f"u{u}p{pe_} yevac")
                if which == 1:
                    hh = p_ // 4
                    s.op("pool", lambda p, hh=hh, par=par, b=b:
                         p.tensor_tensor(
                             out=e_bf[:, hh * 512:(hh + 1) * 512].rearrange(
                                 "p (g c) -> p g c", c=128),
                             in0=xT_bf[par][:, hh * 512:(hh + 1) * 512]
                             .rearrange("p (g c) -> p g c", c=128),
                             in1=v3(o1T_buf[b])[:, hh * 4:hh * 4 + 4, 0:128],
                             op=OP.subtract),
                         reads=[f"xT{par}", f"o1T{b}h{hh}", f"o1Tc{b}"],
                         writes=[f"e_bf{hh}"], dbg=f"u{u}h{hh} E")

        for p_ in range(8):
            if p_ + 2 < 8:
                scores(p_ + 2)
            if p_ >= 1:
                outs(p_ - 1)
        outs(7)
        if which == 1 and first:
            s.op("pool", lambda p: p.memset(
                e_bf[:, :].rearrange("p (g c) -> p g c", c=128)[:, :, 0:1],
                0.0),
                reads=["e_bf0", "e_bf1"], writes=["e_bf0", "e_bf1"],
                dbg=f"u{u} E z0")
        if which == 2:
            states()
        s.op("vector", lambda v: v.tensor_tensor(
            out=sf[:, :], in0=sf[:, :], in1=pdb[:, :], op=OP.add),
            reads=[sf_r, R_PDB], writes=[sf_r], dbg=f"u{u}w{which} sfold")
        s.op("pool", lambda p: p.tensor_copy(
            out=sbf[0:64, :].rearrange("p (g c) -> p g c", c=128)[:, :, 0:64],
            in_=sf[0:64, :].rearrange("p (g c) -> p g c", c=64)),
            reads=[sf_r], writes=[sbf_r], dbg=f"u{u}w{which} scastA")
        s.op("pool", lambda p: p.tensor_copy(
            out=sbf[64:128, :].rearrange(
                "p (g c) -> p g c", c=128)[:, :, 64:128],
            in_=sf[64:128, :].rearrange("p (g c) -> p g c", c=64)),
            reads=[sf_r], writes=[sbf_r], dbg=f"u{u}w{which} scastB")

    def back1a1(u):
        b, c = chunks[u]
        s.op("scalar", lambda sc, b=b: sc.activation(
            out=q2_bf[:, :].rearrange("p (g c) -> p g c", c=128),
            in_=v3(qT_buf[b])[:, :, 0:128], func=AF.Prelu, alpha=50.0,
            scale=0.0025),
            reads=[f"qT{b}", f"qTc{b}"], writes=["q2"], dbg=f"u{u} q2")
        attn_pass(u, 1)

    def back1a2(u):
        b, c = chunks[u]
        transpose_set(lambda f, b=b: k2T_buf[b][:, f * 129 + 0:
                                                f * 129 + 128],
                      lambda sc, ptv: sc.copy(out=k2s_tok[:, :],
                                              in_=ptv[:, :]),
                      reads=[f"k2T{b}", f"k2Tc{b}"], writes=["k2s_tok"],
                      dbg=f"u{u} k2sT", bank=0)
        def eT_half(hh):
            for f in range(4):
                fi = hh * 4 + f
                s.op("tensor", lambda t, fi=fi, hh=hh: t.transpose(
                    PP_BF[1][:, fi * 128:(fi + 1) * 128],
                    e_bf[:, fi * 128:(fi + 1) * 128], ident_sb[:]),
                    reads=[f"e_bf{hh}", "ident"], writes=[R_PP[1]],
                    dbg=f"u{u} eT{fi}")
            s.op("scalar", lambda sc, hh=hh: sc.copy(
                out=e_tok[:, hh * 512:(hh + 1) * 512],
                in_=PP_BF[1][:, hh * 512:(hh + 1) * 512]),
                reads=[R_PP[1]], writes=[f"e_tok{hh}"],
                dbg=f"u{u} eT evac{hh}")
        eT_half(0)
        attn_pass(u, 2, mid=lambda: eT_half(1))

    def backWp(u):
        b, c = chunks[u]
        s3 = u % 3
        for half in range(2):
            bank = half
            for g_ in range(8):
                s.op("tensor", lambda t, g_=g_, half=half, bank=bank:
                     t.matmul(
                         pp[bank][:, :],
                         lhsT=yT_bf[:, g_ * 128:(g_ + 1) * 128],
                         rhs=wbf["p"][:, g_ * 1024 + half * 512:
                                      g_ * 1024 + (half + 1) * 512],
                         start=(g_ == 0), stop=(g_ == 7)),
                     reads=[f"yTh{g_ // 4}", "wp"], writes=[R_PP[bank]],
                     dbg=f"u{u} wp.mm{half}.{g_}")
            s.op("vector", lambda v, half=half, bank=bank, s3=s3:
                 v.scalar_tensor_tensor(
                     out=r1[:, half * 512:(half + 1) * 512],
                     in0=pp[bank][:, :], scalar=1.0,
                     in1=x_tok[s3][:, half * 512:(half + 1) * 512],
                     op0=OP.mult, op1=OP.add,
                     accum_out=st1[:, half:half + 1]),
                 reads=[R_PP[bank], f"x_tok{s3}", f"xpb{s3}"],
                 writes=["r1", "st1"], dbg=f"u{u} r1.{half}")

    def ln_chain(u, st, src_reg, src, out_emit, out_regions, dbg):
        streg = "st1" if st is st1 else "st2"
        s.op("scalar", lambda sc: sc.activation(
            out=yT_bf[:, :], in_=src[:, :], func=AF.Square,
            accum_out=st[:, 2:3]),
            reads=[src_reg], writes=["yTh0", "yTh1", streg + "q"],
            dbg=f"{dbg} sq")
        s.op("scalar", lambda sc: sc.activation(
            out=st[:, 3:4], in_=st[:, 0:1], func=AF.Identity,
            bias=st[:, 1:2]),
            reads=[streg], writes=[streg + "t1"], dbg=f"{dbg} t1")
        s.op("scalar", lambda sc: sc.activation(
            out=st[:, 3:4], in_=st[:, 3:4], func=AF.Identity,
            scale=1.0 / D),
            reads=[streg + "t1"], writes=[streg + "m0"], dbg=f"{dbg} mean")
        s.op("scalar", lambda sc: sc.activation(
            out=st[:, 4:5], in_=st[:, 3:4], func=AF.Square),
            reads=[streg + "m0"], writes=[streg + "ms"], dbg=f"{dbg} msq")
        s.op("scalar", lambda sc: sc.activation(
            out=st[:, 5:6], in_=st[:, 4:5], func=AF.Identity,
            scale=-1.0, bias=epsv[:, 0:1]),
            reads=[streg + "ms", "eps"], writes=[streg + "t2"],
            dbg=f"{dbg} t2")
        s.op("scalar", lambda sc: sc.activation(
            out=st[:, 6:7], in_=st[:, 2:3], func=AF.Ln,
            scale=1.0 / D, bias=st[:, 5:6]),
            reads=[streg + "q", streg + "t2"], writes=[streg + "l"],
            dbg=f"{dbg} ln")
        s.op("scalar", lambda sc: sc.activation(
            out=st[:, 7:8], in_=st[:, 6:7], func=AF.Exp, scale=-0.5),
            reads=[streg + "l"], writes=[streg + "r"], dbg=f"{dbg} rstd")
        s.op("scalar", lambda sc: sc.activation(
            out=st[:, 4:5], in_=st[:, 3:4], func=AF.Identity,
            scale=st[:, 7:8]),
            reads=[streg + "m0", streg + "r"], writes=[streg + "n"],
            dbg=f"{dbg} mr")
        out_emit()

    def back1b_act(u):
        ln_chain(u, st1, "r1", r1, lambda: None, [], dbg=f"u{u} ln1")

    def back1b_norm(u):
        par = u % 2
        s.op("vector", lambda v, par=par: v.tensor_scalar(
            out=r1n[par][:, :], in0=r1[:, :],
            scalar1=st1[:, 7:8], scalar2=st1[:, 4:5],
            op0=OP.mult, op1=OP.subtract),
            reads=["r1", "st1r", "st1n"], writes=[f"r1n{par}"],
            dbg=f"u{u} ln1 norm")

    def back1c(u):
        par = u % 2
        for half in range(2):
            for f in range(4):
                fi = half * 4 + f
                s.op("tensor", lambda t, fi=fi, f=f, par=par: t.transpose(
                    pp[0][:, f * 128:(f + 1) * 128],
                    r1n[par][:, fi * 128:(fi + 1) * 128], ident_f32[:]),
                    reads=[f"r1n{par}", "identf"], writes=[R_PP[0]],
                    dbg=f"u{u} r1nT{fi}")
            s.op("scalar", lambda sc, half=half, par=par: sc.copy(
                out=r1nT_bf[par][:, half * 512:(half + 1) * 512],
                in_=pp[0][:, :]),
                reads=[R_PP[0]], writes=[f"r1nT{par}"],
                dbg=f"u{u} r1nT evac{half}")

    def back2a(u):
        par = u % 2
        s.op("pool", lambda p, par=par: p.tensor_tensor(
            out=r1nb[:, :], in0=r1n[par][:, :], in1=boutb[:, :], op=OP.add),
            reads=[f"r1n{par}", R_BB[5]], writes=["r1nb"], dbg=f"u{u} r1nb")
        for ki in range(8):
            s.op("tensor", lambda t, ki=ki, par=par: t.matmul(
                ps_bank_f32(1)[:, 0:128],
                lhsT=win_bf[:, ki * 128:(ki + 1) * 128],
                rhs=r1nT_bf[par][:, ki * 128:(ki + 1) * 128],
                start=(ki == 0), stop=(ki == 7)),
                reads=["win", f"r1nT{par}"], writes=[R_PS[1]],
                dbg=f"u{u} sdup.{ki}")
        for j in range(8):
            s.op("scalar", lambda sc, j=j: sc.activation(
                out=h_all[:, j * 128:(j + 1) * 128],
                in_=ps_bank_f32(1)[:, 0:128], func=AF.Relu,
                scale=w1p[:, j:j + 1], bias=b1eff[:, j:j + 1]),
                reads=[R_PS[1], "w1p", "b1eff"], writes=["h_all"],
                dbg=f"u{u} h{j}")

    def back2a_g(u):
        for j in range(8):
            if j == 0:
                s.op("vector", lambda v: v.tensor_scalar(
                    out=g32[:, :], in0=h_all[:, 0:128], scalar1=w2p[:, 0:1],
                    scalar2=b2h[:, 0:1], op0=OP.mult, op1=OP.add),
                    reads=["h_all", "w2p", "b2h"], writes=["g32"],
                    dbg=f"u{u} g0")
            else:
                s.op("vector", lambda v, j=j: v.scalar_tensor_tensor(
                    out=g32[:, :], in0=h_all[:, j * 128:(j + 1) * 128],
                    scalar=w2p[:, j:j + 1], in1=g32[:, :],
                    op0=OP.mult, op1=OP.add),
                    reads=["h_all", "w2p", "g32"], writes=["g32"],
                    dbg=f"u{u} g{j}")
        s.op("vector", lambda v: v.tensor_copy(out=g_bf[:, :], in_=g32[:, :]),
             reads=["g32"], writes=["g_bf"], dbg=f"u{u} g cast")

    def back2b(u):
        for half in range(2):
            bank = half
            s.op("tensor", lambda t, half=half, bank=bank: t.matmul(
                pp[bank][:, :], lhsT=g_bf[:, :],
                rhs=wout_bf[:, half * 512:(half + 1) * 512],
                start=True, stop=True),
                reads=["g_bf", "wout"], writes=[R_PP[bank]],
                dbg=f"u{u} ffo{half}")
            s.op("vector", lambda v, half=half, bank=bank:
                 v.scalar_tensor_tensor(
                     out=cent[:, half * 512:(half + 1) * 512],
                     in0=pp[bank][:, :], scalar=1.0,
                     in1=r1nb[:, half * 512:(half + 1) * 512],
                     op0=OP.mult, op1=OP.add,
                     accum_out=st2[:, half:half + 1]),
                 reads=[R_PP[bank], "r1nb"], writes=["stage1", "st2"],
                 dbg=f"u{u} r2.{half}")

    def back2c(u):
        b, c = chunks[u]
        ln_chain(u, st2, "stage1", cent,
                 lambda: s.op("vector", lambda v: v.tensor_scalar(
                     out=o_tok[:, :], in0=cent[:, :],
                     scalar1=st2[:, 7:8], scalar2=st2[:, 4:5],
                     op0=OP.mult, op1=OP.subtract),
                     reads=["stage1", "st2r", "st2n"], writes=["o_tok"],
                     dbg=f"u{u} ln2 norm"),
                 ["o_tok"], dbg=f"u{u} ln2")
        s.dma(lambda g, b=b, c=c: g.dma_start(
            out=out[b, c * 128:(c + 1) * 128, :], in_=o_tok[:, :]),
            reads=["o_tok"], dbg=f"st out u{u}", dsem="st0")

    chunks = [(c_ % nb, c_ // nb) for c_ in range(nb * nch)] if nb > 1 else \
        [(0, c_) for c_ in range(nch)]
    NN = len(chunks)

    xprep(0)
    front(0)
    if NN > 1:
        xprep(1)
    for t in range(NN):
        if t >= 2:
            back2a(t - 2)
        if t >= 1:
            back1b_act(t - 1)
        if t + 1 < NN:
            front(t + 1)
        if t >= 1:
            back1b_norm(t - 1)
            back1c(t - 1)
        if t >= 2:
            back2a_g(t - 2)
        back1a1(t)
        back1a2(t)
        backWp(t)
        if t + 2 < NN:
            xprep(t + 2)
        if t >= 2:
            back2b(t - 2)
        if t >= 2:
            back2c(t - 2)
    back1b_act(NN - 1)
    back1b_norm(NN - 1)
    back1c(NN - 1)
    for u in (NN - 2, NN - 1):
        back2a(u)
        back2a_g(u)
        back2b(u)
        back2c(u)

    s.finalize()
    sem_keys = ["tensor", "vector", "scalar", "pool", "sync",
                "su", "sid", "ws0", "ws1", "ws2", "sx0", "sx1", "sx2",
                "st0", "st1"]
    from contextlib import ExitStack as _ES
    with _ES() as sctx:
        sems = {k: sctx.enter_context(nc.semaphore(f"se_{k}"))
                for k in sem_keys}
        with nc.Block() as block:
            s.emit_all(block, sems)
    ctx.close()
    return nc


def make_in_maps(inputs, nb=B_PER_CORE, ncores=8):
    x = np.ascontiguousarray(inputs["L0_attn_x"], dtype=np.float32)
    brows = np.ascontiguousarray(np.stack([
        inputs["L0_attn_bq"], inputs["L0_attn_bk1"], inputs["L0_attn_bv"],
        inputs["L0_attn_bk2"], inputs["L0_attn_bp"], inputs["L0_ffn_bout"],
    ], axis=0).astype(np.float32))
    common = {
        "w_q": np.ascontiguousarray(inputs["L0_attn_Wq"], dtype=np.float32),
        "w_k1": np.ascontiguousarray(inputs["L0_attn_Wk1"], dtype=np.float32),
        "w_v": np.ascontiguousarray(inputs["L0_attn_Wv"], dtype=np.float32),
        "w_k2": np.ascontiguousarray(inputs["L0_attn_Wk2"], dtype=np.float32),
        "w_p": np.ascontiguousarray(inputs["L0_attn_Wp"], dtype=np.float32),
        "w_in": np.ascontiguousarray(inputs["L0_ffn_Win"], dtype=np.float32),
        "w_out": np.ascontiguousarray(inputs["L0_ffn_Wout"], dtype=np.float32),
        "w1d": np.ascontiguousarray(inputs["L0_ffn_W1"], dtype=np.float32),
        "b1d": np.ascontiguousarray(inputs["L0_ffn_b1"], dtype=np.float32),
        "w2d": np.ascontiguousarray(inputs["L0_ffn_W2"], dtype=np.float32),
        "b2d": np.ascontiguousarray(inputs["L0_ffn_b2"],
                                    dtype=np.float32).reshape(1, 64),
        "bind": np.ascontiguousarray(inputs["L0_ffn_bin_"],
                                     dtype=np.float32).reshape(1, 64),
        "brows": brows,
        "identd": np.eye(128, dtype=ml_dtypes.bfloat16),
        "maskd": np.triu(np.ones((128, 128))).astype(ml_dtypes.bfloat16),
    }
    in_maps = []
    for i in range(ncores):
        m = dict(common)
        m["xin"] = np.ascontiguousarray(x[nb * i: nb * (i + 1)])
        in_maps.append(m)
    return in_maps


def kernel(**inputs):
    nc = build_program()
    in_maps = make_in_maps(inputs)
    res = run_bass_kernel_spmd(nc, in_maps, core_ids=list(range(8)))
    return np.concatenate([r["out"] for r in res.results], axis=0)


# version 42
# speedup vs baseline: 1.0215x; 1.0102x over previous
import numpy as np
import ml_dtypes
import concourse.bass as bass
import concourse.mybir as mybir
from concourse.bass_utils import run_bass_kernel_spmd

F32 = mybir.dt.float32
BF16 = mybir.dt.bfloat16
AF = mybir.ActivationFunctionType
OP = mybir.AluOpType

D = 1024
L = 2048
B_PER_CORE = 2
NCH = L // 128

SAME_ENGINE_WINDOW = 4


class _Op:
    __slots__ = ("eng", "emit", "deps", "need_inc", "is_dma", "count_after",
                 "idx", "dbg", "sem_key")

    def __init__(self, eng, emit, idx, is_dma, dbg, sem_key):
        self.eng = eng
        self.emit = emit
        self.idx = idx
        self.is_dma = is_dma
        self.dbg = dbg
        self.sem_key = sem_key
        self.deps = []
        self.need_inc = is_dma
        self.count_after = None


class Sched:

    ENGS = ("tensor", "vector", "scalar", "pool", "sync")

    def __init__(self):
        self.ops = {e: [] for e in self.ENGS}
        self.reg = {}

    def op(self, eng, emit, reads=(), writes=(), is_dma=False, dbg="",
           sem_key=None):
        o = _Op(eng, emit, len(self.ops[eng]), is_dma, dbg,
                sem_key if sem_key is not None else eng)
        deps = set()
        for r in reads:
            st = self.reg.get(r)
            if st is None:
                continue
            if st["w"] is not None:
                deps.add(st["w"])
            if r.startswith("P!"):
                for rd in st["r"]:
                    if rd.eng != eng:
                        deps.add(rd)
        for w in writes:
            st = self.reg.get(w)
            if st:
                if st["w"] is not None:
                    deps.add(st["w"])
                deps.update(st["r"])
        for d in deps:
            if d is o:
                continue
            if d.eng == eng:
                if eng in ("tensor", "sync"):
                    continue
                if o.idx - d.idx > SAME_ENGINE_WINDOW:
                    continue
            o.deps.append(d)
        for r in reads:
            st = self.reg.setdefault(r, {"w": None, "r": []})
            st["r"].append(o)
        for w in writes:
            self.reg[w] = {"w": o, "r": []}
        self.ops[eng].append(o)
        return o

    def dma(self, emit, dbg="", reads=(), writes=(), dsem="su"):
        return self.op("sync", emit, reads=reads, writes=writes,
                       is_dma=True, dbg=dbg, sem_key=dsem)

    def finalize(self):
        for e in self.ENGS:
            for o in self.ops[e]:
                for d in o.deps:
                    d.need_inc = True
        self.totals = {}
        counts = {}
        for e in self.ENGS:
            for o in self.ops[e]:
                if o.need_inc or o.is_dma:
                    counts[o.sem_key] = counts.get(o.sem_key, 0) + (
                        16 if o.is_dma else 1)
                    o.count_after = counts[o.sem_key]
        self.totals = counts
        for e in self.ENGS:
            for o in self.ops[e]:
                if o.is_dma and o.sem_key == "su":
                    o.count_after = counts.get("su", 0)

    def emit_all(self, block, sems):
        outer = self

        def run_stream(eng_name):
            def f(eng_obj):
                waited = {}
                for o in outer.ops[eng_name]:
                    need = {}
                    for d in o.deps:
                        v = d.count_after
                        assert v is not None, (o.dbg, d.dbg)
                        if need.get(d.sem_key, 0) < v:
                            need[d.sem_key] = v
                    for pk in sorted(need):
                        v = need[pk]
                        if waited.get(pk, 0) < v:
                            eng_obj.wait_ge(sems[pk], v)
                            waited[pk] = v
                    inst = o.emit(eng_obj)
                    if o.need_inc:
                        inst.then_inc(sems[o.sem_key],
                                      16 if o.is_dma else 1)
                if eng_name == "sync":
                    for k, v in sorted(outer.totals.items()):
                        if k not in outer.ENGS and v:
                            eng_obj.wait_ge(sems[k], v)
            return f

        block.sync(run_stream("sync"))
        block.tensor(run_stream("tensor"))
        block.vector(run_stream("vector"))
        block.scalar(run_stream("scalar"))
        block.gpsimd(run_stream("pool"))


def build_program(nb=B_PER_CORE, nch=NCH):
    nc = bass.Bass("TRN2", target_bir_lowering=False)
    s = Sched()

    xin = nc.dram_tensor("xin", [nb, L, D], F32, kind="ExternalInput").ap()
    out = nc.dram_tensor("out", [nb, L, D], F32, kind="ExternalOutput").ap()
    w_q = nc.dram_tensor("w_q", [D, D], F32, kind="ExternalInput").ap()
    w_k1 = nc.dram_tensor("w_k1", [D, D], F32, kind="ExternalInput").ap()
    w_v = nc.dram_tensor("w_v", [D, D], F32, kind="ExternalInput").ap()
    w_k2 = nc.dram_tensor("w_k2", [D, D], F32, kind="ExternalInput").ap()
    w_p = nc.dram_tensor("w_p", [D, D], F32, kind="ExternalInput").ap()
    w_in = nc.dram_tensor("w_in", [D, 64], F32, kind="ExternalInput").ap()
    w_out = nc.dram_tensor("w_out", [64, D], F32, kind="ExternalInput").ap()
    w1d = nc.dram_tensor("w1d", [64, 16], F32, kind="ExternalInput").ap()
    b1d = nc.dram_tensor("b1d", [64, 16], F32, kind="ExternalInput").ap()
    w2d = nc.dram_tensor("w2d", [64, 16], F32, kind="ExternalInput").ap()
    b2d = nc.dram_tensor("b2d", [1, 64], F32, kind="ExternalInput").ap()
    bind = nc.dram_tensor("bind", [1, 64], F32, kind="ExternalInput").ap()
    brows = nc.dram_tensor("brows", [6, D], F32, kind="ExternalInput").ap()
    identd = nc.dram_tensor("identd", [128, 128], BF16, kind="ExternalInput").ap()
    maskd = nc.dram_tensor("maskd", [128, 128], BF16, kind="ExternalInput").ap()

    from contextlib import ExitStack
    ctx = ExitStack()
    e = ctx.enter_context

    stage = [e(nc.sbuf_tensor(f"stage{i}", [128, 1024], F32)) for i in range(2)]
    wbf = {}
    for nm in ("q", "k1", "v", "k2", "p"):
        wbf[nm] = e(nc.sbuf_tensor(f"w{nm}_bf", [128, 8 * 1024], BF16))
    win_bf = e(nc.sbuf_tensor("win_bf", [128, 8 * 128], BF16))
    wout_bf = e(nc.sbuf_tensor("wout_bf", [128, 1024], BF16))
    w1p = e(nc.sbuf_tensor("w1p", [128, 8], F32))
    w2p = e(nc.sbuf_tensor("w2p", [128, 8], F32))
    b1eff = e(nc.sbuf_tensor("b1eff", [128, 8], F32))
    ffn_small = e(nc.sbuf_tensor("ffn_small", [128, 48], F32))
    bin_dup = e(nc.sbuf_tensor("bin_dup", [128, 1], F32))
    b2h = e(nc.sbuf_tensor("b2h", [128, 1], F32))
    epsv = e(nc.sbuf_tensor("epsv", [128, 1], F32))
    ident_sb = e(nc.sbuf_tensor("ident_sb", [128, 128], BF16))
    ident_f32 = e(nc.sbuf_tensor("ident_f32", [128, 128], F32))
    mask_sb = e(nc.sbuf_tensor("mask_sb", [128, 128], BF16))
    bqb = e(nc.sbuf_tensor("bqb", [128, D], BF16))
    bk1b = e(nc.sbuf_tensor("bk1b", [128, D], BF16))
    bvb = e(nc.sbuf_tensor("bvb", [128, D], BF16))
    bk2b = e(nc.sbuf_tensor("bk2b", [128, D], BF16))
    bpb = e(nc.sbuf_tensor("bpb", [128, D], BF16))
    boutb = e(nc.sbuf_tensor("boutb", [128, D], BF16))

    x_tok = [e(nc.sbuf_tensor(f"x_tok{i}", [128, D], F32)) for i in range(3)]
    o_tok = e(nc.sbuf_tensor("o_tok", [128, D], F32))
    xT_bf = [e(nc.sbuf_tensor(f"xT_bf{i}", [128, D], BF16)) for i in range(2)]
    q_tok = e(nc.sbuf_tensor("q_tok", [128, D], BF16))
    k_tok = [e(nc.sbuf_tensor(f"k_tok{i}", [128, D], BF16)) for i in range(2)]
    v_tok = [e(nc.sbuf_tensor(f"v_tok{i}", [128, D], BF16)) for i in range(2)]
    k2_tok = e(nc.sbuf_tensor("k2_tok", [128, D], BF16))
    ztmp = stage[0]
    ze = stage[1]
    kT_bf = [e(nc.sbuf_tensor(f"kT_bf{i}", [128, D], BF16)) for i in range(2)]
    qT_buf = [e(nc.sbuf_tensor(f"qT_buf{i}", [128, 8 * 129], BF16))
              for i in range(nb)]
    k2T_buf = [e(nc.sbuf_tensor(f"k2T_buf{i}", [128, 8 * 129], BF16))
               for i in range(nb)]
    o1T_buf = [e(nc.sbuf_tensor(f"o1T_buf{i}", [128, 8 * 129], BF16))
               for i in range(nb)]
    q2_bf = e(nc.sbuf_tensor("q2_bf", [128, D], BF16))
    e_bf = e(nc.sbuf_tensor("e_bf", [128, D], BF16))
    e_tok = e(nc.sbuf_tensor("e_tok", [128, D], BF16))
    k2s_tok = e(nc.sbuf_tensor("k2s_tok", [128, D], BF16))
    smT = [e(nc.sbuf_tensor(f"smT{i}", [128, 256], BF16)) for i in range(4)]
    yT_bf = e(nc.sbuf_tensor("yT_bf", [128, D], BF16))
    s1_f32 = [e(nc.sbuf_tensor(f"s1_f32{i}", [128, 512], F32))
              for i in range(nb)]
    s2_f32 = [e(nc.sbuf_tensor(f"s2_f32{i}", [128, 512], F32))
              for i in range(nb)]
    s1_bf = [e(nc.sbuf_tensor(f"s1_bf{i}", [128, 1024], BF16))
             for i in range(nb)]
    s2_bf = [e(nc.sbuf_tensor(f"s2_bf{i}", [128, 1024], BF16))
             for i in range(nb)]
    r1 = e(nc.sbuf_tensor("r1", [128, D], F32))
    cent = stage[1]
    r1n = [e(nc.sbuf_tensor(f"r1n{i}", [128, D], F32)) for i in range(2)]
    r1nb = e(nc.sbuf_tensor("r1nb", [128, D], F32))
    r1nT_bf = [e(nc.sbuf_tensor(f"r1nT_bf{i}", [128, D], BF16))
               for i in range(2)]
    h_all = e(nc.sbuf_tensor("h_all", [128, 1024], BF16))
    g32 = e(nc.sbuf_tensor("g32", [128, 128], F32))
    g_bf = e(nc.sbuf_tensor("g_bf", [128, 128], BF16))
    st1 = e(nc.sbuf_tensor("st1", [128, 8], F32))
    st2 = e(nc.sbuf_tensor("st2", [128, 8], F32))
    zrow = e(nc.sbuf_tensor("zrow", [1, 640], BF16))

    pp = [e(nc.psum_tensor(f"pp{i}", [128, 512], F32)) for i in range(2)]
    psb = e(nc.psum_tensor("psb", [128, 2048], F32))
    po = e(nc.psum_tensor("po", [128, 1024], F32))

    R_PP = ["P!pp0", "P!pp1"]
    R_PS = ["P!ps0", "P!ps1", "P!ps2", "P!ps3"]

    PSB_BF = psb[:, :].bitcast(BF16)
    PP_BF = [pp[i][:, :].bitcast(BF16) for i in range(2)]
    PO_BF = po[:, :].bitcast(BF16)

    def ps_bank_f32(bank):
        return psb[:, bank * 512:(bank + 1) * 512]

    def v3(t, c=129):
        return t[:, :].rearrange("p (g c) -> p g c", c=c)

    def ps_pair(par2):
        return psb[:, par2 * 1024:(par2 + 1) * 1024].rearrange(
            "p (g c) -> p g c", c=512)[:, :, 0:128]

    def po_col(p_):
        return (p_ // 4) * 512 + (p_ % 4) * 128

    def R_PO(p_):
        return "P!poA" if p_ < 4 else "P!poB"

    s.dma(lambda g: g.dma_start(out=ident_sb[:], in_=identd[:, :]),
          writes=["ident"], dbg="ld ident", dsem="sid")
    s.op("scalar", lambda sc: sc.copy(out=ident_f32[:, :], in_=ident_sb[:, :]),
         reads=["ident"], writes=["identf"], dbg="identf")
    s.dma(lambda g: g.dma_start(out=mask_sb[:], in_=maskd[:, :]),
          writes=["mask"], dbg="ld mask")
    s.op("vector", lambda v: v.memset(epsv[:, :], 1e-5), writes=["eps"],
         dbg="eps")
    s.op("vector", lambda v: v.memset(zrow[:, :], 0.0), writes=["zrow"],
         dbg="zrow")

    for i, dst in enumerate([bqb, bk1b, bvb, bk2b, bpb, boutb]):
        s.dma(lambda g, i=i: g.dma_start(
            out=stage[1][:, :],
            in_=brows[i:i + 1, :].partition_broadcast(128)),
            writes=["stage1"], dbg=f"ld bb{i}", dsem="ws1")
        s.op("scalar", lambda sc, dst=dst: sc.copy(out=dst[:, :],
                                                   in_=stage[1][:, :]),
             reads=["stage1"], writes=[f"bb{i}"], dbg=f"cast bb{i}")
    R_BB = [f"bb{i}" for i in range(6)]

    for wi, (nm, dram) in enumerate([("q", w_q), ("k1", w_k1), ("v", w_v),
                                     ("k2", w_k2), ("p", w_p)]):
        for ki in range(8):
            par = ki % 2
            s.dma(lambda g, dram=dram, ki=ki, par=par: g.dma_start(
                out=stage[par][:, :], in_=dram[ki * 128:(ki + 1) * 128, :]),
                writes=[f"stage{par}"], dbg=f"ld w{nm}.{ki}",
                dsem=f"ws{par}")
            s.op("scalar",
                 lambda sc, nm=nm, ki=ki, par=par: sc.copy(
                     out=wbf[nm][:, ki * 1024:(ki + 1) * 1024],
                     in_=stage[par][:, :]),
                 reads=[f"stage{par}"], writes=[f"w{nm}"],
                 dbg=f"cast w{nm}.{ki}")

    for ki in range(8):
        par = ki % 2
        s.dma(lambda g, ki=ki, par=par: g.dma_start(
            out=stage[par][:, 0:64], in_=w_in[ki * 128:(ki + 1) * 128, :]),
            writes=[f"stage{par}"], dbg=f"ld win.{ki}", dsem=f"ws{par}")
        s.op("scalar", lambda sc, ki=ki, par=par: sc.copy(
            out=win_bf[:, ki * 128: ki * 128 + 64], in_=stage[par][:, 0:64]),
            reads=[f"stage{par}"], writes=["win"], dbg=f"cast win.{ki}a")
        s.op("scalar", lambda sc, ki=ki, par=par: sc.copy(
            out=win_bf[:, ki * 128 + 64: ki * 128 + 128],
            in_=stage[par][:, 0:64]),
            reads=[f"stage{par}"], writes=["win"], dbg=f"cast win.{ki}b")
    s.dma(lambda g: g.dma_start(out=stage[0][0:64, :], in_=w_out[:, :]),
          writes=["stage0", "stg0A"], dbg="ld wout a", dsem="ws0")
    s.dma(lambda g: g.dma_start(out=stage[0][64:128, :], in_=w_out[:, :]),
          writes=["stage0"], dbg="ld wout b", dsem="ws2")
    s.op("scalar", lambda sc: sc.copy(out=wout_bf[:, :], in_=stage[0][:, :]),
         reads=["stage0", "stg0A"], writes=["wout"], dbg="cast wout")
    for nm, dram, c0 in (("w1", w1d, 0), ("b1", b1d, 16), ("w2", w2d, 32)):
        s.dma(lambda g, dram=dram, c0=c0: g.dma_start(
            out=ffn_small[0:64, c0:c0 + 16], in_=dram[:, :]),
            writes=["ffn_small"], dbg=f"ld {nm} a")
        s.dma(lambda g, dram=dram, c0=c0: g.dma_start(
            out=ffn_small[64:128, c0:c0 + 16], in_=dram[:, :]),
            writes=["ffn_small"], dbg=f"ld {nm} b")
    s.dma(lambda g: g.dma_start(out=bin_dup[0:64, 0:1],
                                in_=bind[0:1, :].transpose([1, 0])),
          writes=["bin"], dbg="ld bin a")
    s.dma(lambda g: g.dma_start(out=bin_dup[64:128, 0:1],
                                in_=bind[0:1, :].transpose([1, 0])),
          writes=["bin"], dbg="ld bin b")
    s.dma(lambda g: g.dma_start(out=b2h[0:64, 0:1],
                                in_=b2d[0:1, :].transpose([1, 0])),
          writes=["b2h"], dbg="ld b2 a")
    s.dma(lambda g: g.dma_start(out=b2h[64:128, 0:1],
                                in_=b2d[0:1, :].transpose([1, 0])),
          writes=["b2h"], dbg="ld b2 b")
    s.op("vector", lambda v: v.tensor_copy(out=w1p[0:64, :],
                                           in_=ffn_small[0:64, 0:16:2]),
         reads=["ffn_small"], writes=["w1p"], dbg="w1p a")
    s.op("vector", lambda v: v.tensor_copy(out=w1p[64:128, :],
                                           in_=ffn_small[64:128, 1:16:2]),
         reads=["ffn_small"], writes=["w1p"], dbg="w1p b")
    s.op("vector", lambda v: v.tensor_copy(out=w2p[0:64, :],
                                           in_=ffn_small[0:64, 32:48:2]),
         reads=["ffn_small"], writes=["w2p"], dbg="w2p a")
    s.op("vector", lambda v: v.tensor_copy(out=w2p[64:128, :],
                                           in_=ffn_small[64:128, 33:48:2]),
         reads=["ffn_small"], writes=["w2p"], dbg="w2p b")
    s.op("vector", lambda v: v.tensor_copy(out=b1eff[0:64, :],
                                           in_=ffn_small[0:64, 16:32:2]),
         reads=["ffn_small"], writes=["b1eff"], dbg="b1eff a")
    s.op("vector", lambda v: v.tensor_copy(out=b1eff[64:128, :],
                                           in_=ffn_small[64:128, 17:32:2]),
         reads=["ffn_small"], writes=["b1eff"], dbg="b1eff b")
    s.op("vector", lambda v: v.scalar_tensor_tensor(
        out=b1eff[:, :], in0=w1p[:, :], scalar=bin_dup[:, 0:1],
        in1=b1eff[:, :], op0=OP.mult, op1=OP.add),
        reads=["w1p", "bin", "b1eff"], writes=["b1eff"], dbg="b1eff fold")
    s.op("vector", lambda v: v.tensor_scalar(out=b2h[:, :], in0=b2h[:, :],
                                             scalar1=0.5, scalar2=None,
                                             op0=OP.mult),
         reads=["b2h"], writes=["b2h"], dbg="b2 half")

    s.dma(lambda g: g.dma_start(out=x_tok[0][:, :], in_=xin[0, 0:128, :]),
          writes=["x_tok0"], dbg="ld x b0c0", dsem="sx0")

    def transpose_set(src_fn, dst_emit, reads, writes, dbg, bank):
        if bank in (0, 1, 2, 3):
            ptv = PSB_BF[:, bank * 1024:(bank + 1) * 1024]
            breg = R_PS[bank]
        elif bank == "poA":
            ptv = PO_BF[:, 0:1024]
            breg = "P!poA"
        else:
            i = int(bank[-1])
            ptv = PP_BF[i]
            breg = R_PP[i]
        for f in range(8):
            s.op("tensor",
                 lambda t, f=f, ptv=ptv: t.transpose(
                     ptv[:, f * 128:(f + 1) * 128], src_fn(f), ident_sb[:]),
                 reads=list(reads) + ["ident"], writes=[breg],
                 dbg=f"{dbg} T{f}")
        s.op("scalar", lambda sc, ptv=ptv: dst_emit(sc, ptv),
             reads=[breg], writes=list(writes), dbg=f"{dbg} evac")

    PBANKS = [
        (pp[0][:, :], R_PP[0]),
        (pp[1][:, :], R_PP[1]),
        (po[:, 0:512], "P!poA"),
        (po[:, 512:1024], "P!poB"),
    ]

    def projection(wname, out_emit, out_regions, dbg, xpar=0,
                   extra_reads=(), pbank=(0, 1)):
        for half in range(2):
            bview, breg = PBANKS[pbank[half]]
            for ki in range(8):
                s.op("tensor",
                     lambda t, ki=ki, half=half, bview=bview, wname=wname,
                     xpar=xpar: t.matmul(
                         bview,
                         lhsT=xT_bf[xpar][:, ki * 128:(ki + 1) * 128],
                         rhs=wbf[wname][:, ki * 1024 + half * 512:
                                        ki * 1024 + (half + 1) * 512],
                         start=(ki == 0), stop=(ki == 7)),
                     reads=[f"xT{xpar}", f"w{wname}"], writes=[breg],
                     dbg=f"{dbg}.mm{half}.{ki}")
            s.op("vector",
                 lambda v, half=half, bview=bview: out_emit(v, half, bview),
                 reads=[breg] + list(extra_reads),
                 writes=list(out_regions), dbg=f"{dbg}.evac{half}")

    def xprep(u):
        b, c = chunks[u]
        s3 = u % 3
        par = u % 2
        for half in range(2):
            bk = 3 if half == 0 else 2
            for f in range(4):
                fi = half * 4 + f
                s.op("tensor", lambda t, fi=fi, f=f, s3=s3, bk=bk: t.transpose(
                    ps_bank_f32(bk)[:, f * 128:(f + 1) * 128],
                    x_tok[s3][:, fi * 128:(fi + 1) * 128], ident_f32[:]),
                    reads=[f"x_tok{s3}", "identf"], writes=[R_PS[bk]],
                    dbg=f"u{u} xT{fi}")
            s.op("scalar", lambda sc, half=half, par=par, bk=bk: sc.copy(
                out=xT_bf[par][:, half * 512:(half + 1) * 512],
                in_=ps_bank_f32(bk)[:, :]),
                reads=[R_PS[bk]], writes=[f"xT{par}"],
                dbg=f"u{u} xT evac{half}")

    def front(u):
        b, c = chunks[u]
        par = u % 2
        s3 = u % 3
        first = (c == 0)
        XR = f"x_tok{s3}"

        if u + 1 < NN:
            nb_, nc_ = chunks[u + 1]
            ns3 = (u + 1) % 3
            s.dma(lambda g, nb_=nb_, nc_=nc_, ns3=ns3: g.dma_start(
                out=x_tok[ns3][:, :],
                in_=xin[nb_, nc_ * 128:(nc_ + 1) * 128, :]),
                writes=[f"x_tok{ns3}"], dbg=f"ld x u{u+1}",
                dsem=f"sx{ns3}")

        s.op("pool", lambda p, s3=s3: p.tensor_tensor(
            out=x_tok[s3][:, :], in0=x_tok[s3][:, :], in1=bpb[:, :],
            op=OP.add),
            reads=[XR, R_BB[4]], writes=[XR, f"xpb{s3}"], dbg=f"u{u} xpb")

        if first:
            s.op("pool", lambda p, b=b: p.memset(v3(qT_buf[b])[:, :, 0:1], 0.0),
                 reads=[f"qT{b}"], writes=[f"qTc{b}"], dbg=f"b{b} zero qTc")
            s.op("pool", lambda p, b=b: p.memset(v3(k2T_buf[b])[:, :, 0:1], 0.0),
                 reads=[f"k2T{b}"], writes=[f"k2Tc{b}"], dbg=f"b{b} zero k2Tc")
            s.op("pool", lambda p, b=b: p.memset(v3(o1T_buf[b])[:, :, 0:1], 0.0),
                 reads=[f"o1T{b}h0", f"o1T{b}h1"], writes=[f"o1Tc{b}"],
                 dbg=f"b{b} zero o1Tc")
            s.op("vector", lambda v, b=b: v.memset(s1_f32[b][:, :], 0.0),
                 writes=[f"s1f{b}"], dbg=f"b{b} zero s1f")
            s.op("vector", lambda v, b=b: v.memset(s2_f32[b][:, :], 0.0),
                 writes=[f"s2f{b}"], dbg=f"b{b} zero s2f")
            s.op("vector", lambda v, b=b: v.memset(s1_bf[b][:, :], 0.0),
                 writes=[f"s1bf{b}"], dbg=f"b{b} zero s1bf")
            s.op("vector", lambda v, b=b: v.memset(s2_bf[b][:, :], 0.0),
                 writes=[f"s2bf{b}"], dbg=f"b{b} zero s2bf")
        else:
            s.op("pool", lambda p, b=b: p.tensor_copy(
                out=v3(qT_buf[b])[:, :, 0:1],
                in_=v3(qT_buf[b])[:, :, 128:129]),
                reads=[f"qT{b}"], writes=[f"qTc{b}"], dbg=f"u{u} carry qT")
            s.op("pool", lambda p, b=b: p.tensor_copy(
                out=v3(k2T_buf[b])[:, :, 0:1],
                in_=v3(k2T_buf[b])[:, :, 128:129]),
                reads=[f"k2T{b}"], writes=[f"k2Tc{b}"], dbg=f"u{u} carry k2T")
            s.op("pool", lambda p, b=b: p.tensor_copy(
                out=v3(o1T_buf[b])[:, :, 0:1],
                in_=v3(o1T_buf[b])[:, :, 128:129]),
                reads=[f"o1T{b}h0", f"o1T{b}h1"], writes=[f"o1Tc{b}"],
                dbg=f"u{u} carry o1T")

        projection("q",
                   lambda v, half, bank: v.tensor_tensor(
                       out=q_tok[:, half * 512:(half + 1) * 512],
                       in0=bank[:, :],
                       in1=bqb[:, half * 512:(half + 1) * 512], op=OP.add),
                   ["q_tok"], dbg=f"u{u} Q", xpar=par,
                   extra_reads=[R_BB[0]], pbank=(0, 1))
        projection("k2",
                   lambda v, half, bank: v.tensor_tensor(
                       out=ztmp[:, half * 512:(half + 1) * 512],
                       in0=bank[:, :],
                       in1=bk2b[:, half * 512:(half + 1) * 512], op=OP.add),
                   ["stage0"], dbg=f"u{u} z", xpar=par,
                   extra_reads=[R_BB[3]], pbank=(2, 3))
        projection("k1",
                   lambda v, half, bank, par=par: v.tensor_tensor(
                       out=k_tok[par][:, half * 512:(half + 1) * 512],
                       in0=bank[:, :],
                       in1=bk1b[:, half * 512:(half + 1) * 512], op=OP.add),
                   [f"k_tok{par}"], dbg=f"u{u} K", xpar=par,
                   extra_reads=[R_BB[1]], pbank=(0, 1))
        projection("v",
                   lambda v, half, bank, par=par: v.tensor_tensor(
                       out=v_tok[par][:, half * 512:(half + 1) * 512],
                       in0=bank[:, :],
                       in1=bvb[:, half * 512:(half + 1) * 512], op=OP.add),
                   [f"v_tok{par}"], dbg=f"u{u} V", xpar=par,
                   extra_reads=[R_BB[2]], pbank=(2, 3))

        s.op("scalar", lambda sc: sc.activation(
            out=ze[:, :], in_=ztmp[:, :], func=AF.Exp, scale=-0.02 / 32.0),
            reads=["stage0"], writes=["stage1"], dbg=f"u{u} zexp")
        def _zk2(v):
            with nc.allow_low_precision(reason="bf16 operand rounding"):
                return v.tensor_scalar(
                    out=k2_tok[:, :], in0=ze[:, :], scalar1=-0.25,
                    scalar2=0.75, op0=OP.mult, op1=OP.add)
        s.op("vector", _zk2,
             reads=["stage1"], writes=["k2_tok"], dbg=f"u{u} zk2")

        transpose_set(lambda f: q_tok[:, f * 128:(f + 1) * 128],
                      lambda sc, ptv, b=b: sc.copy(
                          out=v3(qT_buf[b])[:, :, 1:129],
                          in_=ptv[:, :].rearrange("p (g c) -> p g c", c=128)),
                      reads=["q_tok"], writes=[f"qT{b}"], dbg=f"u{u} qT",
                      bank="pp0")
        transpose_set(lambda f, par=par: k_tok[par][:, f * 128:(f + 1) * 128],
                      lambda sc, ptv, par=par: sc.copy(out=kT_bf[par][:, :],
                                                       in_=ptv[:, :]),
                      reads=[f"k_tok{par}"], writes=[f"kT{par}"],
                      dbg=f"u{u} kT", bank="pp1")
        transpose_set(lambda f: k2_tok[:, f * 128:(f + 1) * 128],
                      lambda sc, ptv, b=b: sc.copy(
                          out=v3(k2T_buf[b])[:, :, 1:129],
                          in_=ptv[:, :].rearrange("p (g c) -> p g c", c=128)),
                      reads=["k2_tok"], writes=[f"k2T{b}"], dbg=f"u{u} k2T",
                      bank="poA")

    def attn_pass(u, which, mid=None):
        b, c = chunks[u]
        par = u % 2
        first = (c == 0)
        if which == 1:
            lhsT_s = lambda p_, h, par=par: kT_bf[par][64 * h:64 * h + 64,
                                                       p_ * 128:(p_ + 1) * 128]
            rhs_q = lambda p_, h, b=b: qT_buf[b][64 * h:64 * h + 64,
                                                 p_ * 129 + 1: p_ * 129 + 129]
            sc_reads = [f"kT{par}", f"qT{b}"]
            val, val_reads = v_tok[par], [f"v_tok{par}"]
            key_tok, key_reads = k_tok[par], [f"k_tok{par}"]
            sbf, sbf_r = s1_bf[b], f"s1bf{b}"
            sf, sf_r = s1_f32[b], f"s1f{b}"
            pdb, R_PDB = pp[0], R_PP[0]
        else:
            lhsT_s = lambda p_, h, b=b: k2T_buf[b][64 * h:64 * h + 64,
                                                   p_ * 129 + 0: p_ * 129 + 128]
            rhs_q = lambda p_, h: q2_bf[64 * h:64 * h + 64,
                                        p_ * 128:(p_ + 1) * 128]
            sc_reads = [f"k2T{b}", f"k2Tc{b}", "q2"]
            val, val_reads = e_tok, ["e_tokH"]
            key_tok, key_reads = k2s_tok, ["k2s_tok"]
            sbf, sbf_r = s2_bf[b], f"s2bf{b}"
            sf, sf_r = s2_f32[b], f"s2f{b}"
            pdb, R_PDB = pp[1], R_PP[1]

        def scores(p_):
            P2 = (p_ + (0 if which == 1 else 1)) % 2
            S4 = p_ % 4
            s.op("tensor", lambda t, p_=p_, P2=P2: t.matmul(
                ps_bank_f32(2 * P2)[:, 0:128], lhsT=lhsT_s(p_, 0),
                rhs=rhs_q(p_, 0), start=True, stop=True),
                reads=sc_reads, writes=[R_PS[2 * P2]],
                dbg=f"u{u}w{which}p{p_} sce")
            s.op("tensor", lambda t, p_=p_, P2=P2: t.matmul(
                ps_bank_f32(2 * P2 + 1)[:, 0:128], lhsT=lhsT_s(p_, 1),
                rhs=rhs_q(p_, 1), start=True, stop=True),
                reads=sc_reads, writes=[R_PS[2 * P2 + 1]],
                dbg=f"u{u}w{which}p{p_} sco")
            s.op("vector", lambda v, p_=p_, P2=P2, S4=p_ % 4: v.tensor_tensor(
                out=smT[S4][:, :].rearrange("p (g c) -> p g c", c=128),
                in0=ps_pair(P2),
                in1=mask_sb[:, :].unsqueeze(1).broadcast_to([128, 2, 128]),
                op=OP.mult),
                reads=[R_PS[2 * P2], R_PS[2 * P2 + 1], "mask"],
                writes=[f"smT{p_ % 4}"], dbg=f"u{u}w{which}p{p_} mask")

        def states():
            for p_ in range(8):
                s.op("tensor", lambda t, p_=p_: t.matmul(
                    pdb[0:64, p_ * 64:(p_ + 1) * 64],
                    lhsT=key_tok[:, p_ * 128: p_ * 128 + 64],
                    rhs=val[:, p_ * 128: p_ * 128 + 64],
                    start=True, stop=True),
                    reads=key_reads + (["e_tok0", "e_tok1"]
                                       if which == 2 else list(val_reads)),
                    writes=[R_PDB],
                    dbg=f"u{u}w{which}p{p_} ste")
                s.op("tensor", lambda t, p_=p_: t.matmul(
                    pdb[64:128, p_ * 64:(p_ + 1) * 64],
                    lhsT=key_tok[:, p_ * 128 + 64: p_ * 128 + 128],
                    rhs=val[:, p_ * 128 + 64: p_ * 128 + 128],
                    start=True, stop=True, tile_position=(0, 64)),
                    reads=key_reads + (["e_tok0", "e_tok1"]
                                       if which == 2 else list(val_reads)),
                    writes=[R_PDB],
                    dbg=f"u{u}w{which}p{p_} sto")

        def dummies():
            for bi in range(2):
                s.op("tensor", lambda t, bi=bi: t.matmul(
                    po[:, bi * 512:(bi + 1) * 512], lhsT=zrow[0:1, 0:128],
                    rhs=zrow[0:1, 128:640], start=True, stop=False,
                    skip_group_check=True),
                    reads=["zrow"],
                    writes=["P!poA" if bi == 0 else "P!poB"],
                    dbg=f"u{u}w{which} ydz{bi}")

        dummies()
        scores(0)
        scores(1)
        scores(2)
        if which == 1:
            states()
        if mid is not None:
            mid()
        if which == 2:
            for bi in range(2):
                s.op("tensor", lambda t, bi=bi, b=b: t.matmul(
                    po[:, bi * 512:(bi + 1) * 512], lhsT=ident_sb[:, :],
                    rhs=v3(o1T_buf[b])[:, bi * 4:(bi + 1) * 4, 1:129],
                    start=False, stop=False, skip_group_check=True),
                    reads=[f"o1T{b}h{bi}", "ident"],
                    writes=["P!poA" if bi == 0 else "P!poB"],
                    dbg=f"u{u}w{which} oid{bi}")
        def outs(p_):
            cb = po_col(p_)
            rp = R_PO(p_)
            last = (p_ % 4 == 3)
            oie_reads = ([f"e_tok{p_ // 4}"] if which == 2
                         else list(val_reads))
            s.op("tensor", lambda t, p_=p_, S4=p_ % 4, cb=cb: t.matmul(
                po[0:64, cb:cb + 128], lhsT=val[:, p_ * 128: p_ * 128 + 64],
                rhs=smT[S4][:, 0:128], start=False, stop=False,
                skip_group_check=True),
                reads=oie_reads + [f"smT{p_ % 4}"], writes=[rp],
                dbg=f"u{u}w{which}p{p_} oie")
            s.op("tensor", lambda t, p_=p_, S4=p_ % 4, cb=cb: t.matmul(
                po[64:128, cb:cb + 128],
                lhsT=val[:, p_ * 128 + 64: p_ * 128 + 128],
                rhs=smT[S4][:, 128:256], start=False, stop=False,
                tile_position=(0, 64), skip_group_check=True),
                reads=oie_reads + [f"smT{p_ % 4}"], writes=[rp],
                dbg=f"u{u}w{which}p{p_} oio")
            if which == 1:
                rhs_pair = (lambda p_=p_, b=b:
                            qT_buf[b][:, p_ * 129 + 1: p_ * 129 + 129])
            else:
                rhs_pair = (lambda p_=p_:
                            q2_bf[:, p_ * 128:(p_ + 1) * 128])
            s.op("tensor", lambda t, p_=p_, cb=cb, last=last,
                 rhs_pair=rhs_pair: t.matmul(
                po[:, cb:cb + 128], lhsT=sbf[:, p_ * 128:(p_ + 1) * 128],
                rhs=rhs_pair(), start=False, stop=last,
                skip_group_check=True),
                reads=[sbf_r] + sc_reads, writes=[rp],
                dbg=f"u{u}w{which}p{p_} osd")
            if last:
                bb = p_ // 4
                rpe = R_PO(p_)
                for pe_ in range(p_ - 3, p_ + 1):
                    ce = po_col(pe_)
                    if which == 1:
                        ee = ("scalar" if (bb == 0 or pe_ % 2 == 0)
                              else "vector")
                        def _emit_o1(x, pe_=pe_, ce=ce, b=b):
                            if hasattr(x, 'copy'):
                                return x.copy(
                                    out=o1T_buf[b][:, pe_ * 129 + 1:
                                                   pe_ * 129 + 129],
                                    in_=po[:, ce:ce + 128])
                            return x.tensor_copy(
                                out=o1T_buf[b][:, pe_ * 129 + 1:
                                               pe_ * 129 + 129],
                                in_=po[:, ce:ce + 128])
                        s.op(ee, _emit_o1,
                             reads=[rpe], writes=[f"o1T{b}h{bb}"],
                             dbg=f"u{u}p{pe_} o1evac")
                    else:
                        ee = ("scalar" if (bb == 0 or pe_ % 2 == 0)
                              else "vector")
                        def _emit_y(x, pe_=pe_, ce=ce):
                            if hasattr(x, 'copy'):
                                return x.copy(
                                    out=yT_bf[:, pe_ * 128:(pe_ + 1) * 128],
                                    in_=po[:, ce:ce + 128])
                            return x.tensor_copy(
                                out=yT_bf[:, pe_ * 128:(pe_ + 1) * 128],
                                in_=po[:, ce:ce + 128])
                        s.op(ee, _emit_y,
                             reads=[rpe], writes=[f"yTh{bb}"],
                             dbg=f"u{u}p{pe_} yevac")
                if which == 1:
                    hh = p_ // 4
                    s.op("pool", lambda p, hh=hh, par=par, b=b:
                         p.tensor_tensor(
                             out=e_bf[:, hh * 512:(hh + 1) * 512].rearrange(
                                 "p (g c) -> p g c", c=128),
                             in0=xT_bf[par][:, hh * 512:(hh + 1) * 512]
                             .rearrange("p (g c) -> p g c", c=128),
                             in1=v3(o1T_buf[b])[:, hh * 4:hh * 4 + 4, 0:128],
                             op=OP.subtract),
                         reads=[f"xT{par}", f"o1T{b}h{hh}", f"o1Tc{b}"],
                         writes=[f"e_bf{hh}"], dbg=f"u{u}h{hh} E")

        for p_ in range(8):
            if p_ >= 1:
                outs(p_ - 1)
            if p_ + 3 < 8:
                scores(p_ + 3)
        outs(7)
        if which == 1 and first:
            s.op("pool", lambda p: p.memset(
                e_bf[:, :].rearrange("p (g c) -> p g c", c=128)[:, :, 0:1],
                0.0),
                reads=["e_bf0", "e_bf1"], writes=["e_bf0", "e_bf1"],
                dbg=f"u{u} E z0")
        if which == 2:
            states()
        s.op("vector", lambda v: v.tensor_tensor(
            out=sf[:, :], in0=sf[:, :], in1=pdb[:, :], op=OP.add),
            reads=[sf_r, R_PDB], writes=[sf_r], dbg=f"u{u}w{which} sfold")
        s.op("pool", lambda p: p.tensor_copy(
            out=sbf[0:64, :].rearrange("p (g c) -> p g c", c=128)[:, :, 0:64],
            in_=sf[0:64, :].rearrange("p (g c) -> p g c", c=64)),
            reads=[sf_r], writes=[sbf_r], dbg=f"u{u}w{which} scastA")
        s.op("pool", lambda p: p.tensor_copy(
            out=sbf[64:128, :].rearrange(
                "p (g c) -> p g c", c=128)[:, :, 64:128],
            in_=sf[64:128, :].rearrange("p (g c) -> p g c", c=64)),
            reads=[sf_r], writes=[sbf_r], dbg=f"u{u}w{which} scastB")

    def back1a1(u):
        b, c = chunks[u]
        s.op("scalar", lambda sc, b=b: sc.activation(
            out=q2_bf[:, :].rearrange("p (g c) -> p g c", c=128),
            in_=v3(qT_buf[b])[:, :, 0:128], func=AF.Prelu, alpha=50.0,
            scale=0.0025),
            reads=[f"qT{b}", f"qTc{b}"], writes=["q2"], dbg=f"u{u} q2")
        attn_pass(u, 1)

    def back1a2(u):
        b, c = chunks[u]
        transpose_set(lambda f, b=b: k2T_buf[b][:, f * 129 + 0:
                                                f * 129 + 128],
                      lambda sc, ptv: sc.copy(out=k2s_tok[:, :],
                                              in_=ptv[:, :]),
                      reads=[f"k2T{b}", f"k2Tc{b}"], writes=["k2s_tok"],
                      dbg=f"u{u} k2sT", bank=0)
        def eT_half(hh):
            for f in range(4):
                fi = hh * 4 + f
                s.op("tensor", lambda t, fi=fi, hh=hh: t.transpose(
                    PP_BF[1][:, fi * 128:(fi + 1) * 128],
                    e_bf[:, fi * 128:(fi + 1) * 128], ident_sb[:]),
                    reads=[f"e_bf{hh}", "ident"], writes=[R_PP[1]],
                    dbg=f"u{u} eT{fi}")
            s.op("scalar", lambda sc, hh=hh: sc.copy(
                out=e_tok[:, hh * 512:(hh + 1) * 512],
                in_=PP_BF[1][:, hh * 512:(hh + 1) * 512]),
                reads=[R_PP[1]], writes=[f"e_tok{hh}"],
                dbg=f"u{u} eT evac{hh}")
        eT_half(0)
        attn_pass(u, 2, mid=lambda: eT_half(1))

    def backWp(u):
        b, c = chunks[u]
        s3 = u % 3
        for half in range(2):
            bank = half
            for g_ in range(8):
                s.op("tensor", lambda t, g_=g_, half=half, bank=bank:
                     t.matmul(
                         pp[bank][:, :],
                         lhsT=yT_bf[:, g_ * 128:(g_ + 1) * 128],
                         rhs=wbf["p"][:, g_ * 1024 + half * 512:
                                      g_ * 1024 + (half + 1) * 512],
                         start=(g_ == 0), stop=(g_ == 7)),
                     reads=[f"yTh{g_ // 4}", "wp"], writes=[R_PP[bank]],
                     dbg=f"u{u} wp.mm{half}.{g_}")
            s.op("vector", lambda v, half=half, bank=bank, s3=s3:
                 v.scalar_tensor_tensor(
                     out=r1[:, half * 512:(half + 1) * 512],
                     in0=pp[bank][:, :], scalar=1.0,
                     in1=x_tok[s3][:, half * 512:(half + 1) * 512],
                     op0=OP.mult, op1=OP.add,
                     accum_out=st1[:, half:half + 1]),
                 reads=[R_PP[bank], f"x_tok{s3}", f"xpb{s3}"],
                 writes=["r1", "st1"], dbg=f"u{u} r1.{half}")

    def ln_chain(u, st, src_reg, src, out_emit, out_regions, dbg):
        streg = "st1" if st is st1 else "st2"
        s.op("scalar", lambda sc: sc.activation(
            out=yT_bf[:, :], in_=src[:, :], func=AF.Square,
            accum_out=st[:, 2:3]),
            reads=[src_reg], writes=["yTh0", "yTh1", streg + "q"],
            dbg=f"{dbg} sq")
        s.op("scalar", lambda sc: sc.activation(
            out=st[:, 3:4], in_=st[:, 0:1], func=AF.Identity,
            bias=st[:, 1:2]),
            reads=[streg], writes=[streg + "t1"], dbg=f"{dbg} t1")
        s.op("scalar", lambda sc: sc.activation(
            out=st[:, 3:4], in_=st[:, 3:4], func=AF.Identity,
            scale=1.0 / D),
            reads=[streg + "t1"], writes=[streg + "m0"], dbg=f"{dbg} mean")
        s.op("scalar", lambda sc: sc.activation(
            out=st[:, 4:5], in_=st[:, 3:4], func=AF.Square),
            reads=[streg + "m0"], writes=[streg + "ms"], dbg=f"{dbg} msq")
        s.op("scalar", lambda sc: sc.activation(
            out=st[:, 5:6], in_=st[:, 4:5], func=AF.Identity,
            scale=-1.0, bias=epsv[:, 0:1]),
            reads=[streg + "ms", "eps"], writes=[streg + "t2"],
            dbg=f"{dbg} t2")
        s.op("scalar", lambda sc: sc.activation(
            out=st[:, 6:7], in_=st[:, 2:3], func=AF.Ln,
            scale=1.0 / D, bias=st[:, 5:6]),
            reads=[streg + "q", streg + "t2"], writes=[streg + "l"],
            dbg=f"{dbg} ln")
        s.op("scalar", lambda sc: sc.activation(
            out=st[:, 7:8], in_=st[:, 6:7], func=AF.Exp, scale=-0.5),
            reads=[streg + "l"], writes=[streg + "r"], dbg=f"{dbg} rstd")
        s.op("scalar", lambda sc: sc.activation(
            out=st[:, 4:5], in_=st[:, 3:4], func=AF.Identity,
            scale=st[:, 7:8]),
            reads=[streg + "m0", streg + "r"], writes=[streg + "n"],
            dbg=f"{dbg} mr")
        out_emit()

    def back1b_act(u):
        ln_chain(u, st1, "r1", r1, lambda: None, [], dbg=f"u{u} ln1")

    def back1b_norm(u):
        par = u % 2
        s.op("vector", lambda v, par=par: v.tensor_scalar(
            out=r1n[par][:, :], in0=r1[:, :],
            scalar1=st1[:, 7:8], scalar2=st1[:, 4:5],
            op0=OP.mult, op1=OP.subtract),
            reads=["r1", "st1r", "st1n"], writes=[f"r1n{par}"],
            dbg=f"u{u} ln1 norm")

    def back1c(u):
        par = u % 2
        for half in range(2):
            for f in range(4):
                fi = half * 4 + f
                s.op("tensor", lambda t, fi=fi, f=f, par=par: t.transpose(
                    pp[0][:, f * 128:(f + 1) * 128],
                    r1n[par][:, fi * 128:(fi + 1) * 128], ident_f32[:]),
                    reads=[f"r1n{par}", "identf"], writes=[R_PP[0]],
                    dbg=f"u{u} r1nT{fi}")
            s.op("scalar", lambda sc, half=half, par=par: sc.copy(
                out=r1nT_bf[par][:, half * 512:(half + 1) * 512],
                in_=pp[0][:, :]),
                reads=[R_PP[0]], writes=[f"r1nT{par}"],
                dbg=f"u{u} r1nT evac{half}")

    def back2a(u):
        par = u % 2
        s.op("pool", lambda p, par=par: p.tensor_tensor(
            out=r1nb[:, :], in0=r1n[par][:, :], in1=boutb[:, :], op=OP.add),
            reads=[f"r1n{par}", R_BB[5]], writes=["r1nb"], dbg=f"u{u} r1nb")
        for ki in range(8):
            s.op("tensor", lambda t, ki=ki, par=par: t.matmul(
                ps_bank_f32(1)[:, 0:128],
                lhsT=win_bf[:, ki * 128:(ki + 1) * 128],
                rhs=r1nT_bf[par][:, ki * 128:(ki + 1) * 128],
                start=(ki == 0), stop=(ki == 7)),
                reads=["win", f"r1nT{par}"], writes=[R_PS[1]],
                dbg=f"u{u} sdup.{ki}")
        for j in range(8):
            s.op("scalar", lambda sc, j=j: sc.activation(
                out=h_all[:, j * 128:(j + 1) * 128],
                in_=ps_bank_f32(1)[:, 0:128], func=AF.Relu,
                scale=w1p[:, j:j + 1], bias=b1eff[:, j:j + 1]),
                reads=[R_PS[1], "w1p", "b1eff"], writes=["h_all"],
                dbg=f"u{u} h{j}")

    def back2a_g(u):
        for j in range(8):
            if j == 0:
                s.op("vector", lambda v: v.tensor_scalar(
                    out=g32[:, :], in0=h_all[:, 0:128], scalar1=w2p[:, 0:1],
                    scalar2=b2h[:, 0:1], op0=OP.mult, op1=OP.add),
                    reads=["h_all", "w2p", "b2h"], writes=["g32"],
                    dbg=f"u{u} g0")
            else:
                s.op("vector", lambda v, j=j: v.scalar_tensor_tensor(
                    out=g32[:, :], in0=h_all[:, j * 128:(j + 1) * 128],
                    scalar=w2p[:, j:j + 1], in1=g32[:, :],
                    op0=OP.mult, op1=OP.add),
                    reads=["h_all", "w2p", "g32"], writes=["g32"],
                    dbg=f"u{u} g{j}")
        s.op("vector", lambda v: v.tensor_copy(out=g_bf[:, :], in_=g32[:, :]),
             reads=["g32"], writes=["g_bf"], dbg=f"u{u} g cast")

    def back2b(u):
        for half in range(2):
            bank = half
            s.op("tensor", lambda t, half=half, bank=bank: t.matmul(
                pp[bank][:, :], lhsT=g_bf[:, :],
                rhs=wout_bf[:, half * 512:(half + 1) * 512],
                start=True, stop=True),
                reads=["g_bf", "wout"], writes=[R_PP[bank]],
                dbg=f"u{u} ffo{half}")
            s.op("vector", lambda v, half=half, bank=bank:
                 v.scalar_tensor_tensor(
                     out=cent[:, half * 512:(half + 1) * 512],
                     in0=pp[bank][:, :], scalar=1.0,
                     in1=r1nb[:, half * 512:(half + 1) * 512],
                     op0=OP.mult, op1=OP.add,
                     accum_out=st2[:, half:half + 1]),
                 reads=[R_PP[bank], "r1nb"], writes=["stage1", "st2"],
                 dbg=f"u{u} r2.{half}")

    def back2c(u):
        b, c = chunks[u]
        ln_chain(u, st2, "stage1", cent,
                 lambda: s.op("vector", lambda v: v.tensor_scalar(
                     out=o_tok[:, :], in0=cent[:, :],
                     scalar1=st2[:, 7:8], scalar2=st2[:, 4:5],
                     op0=OP.mult, op1=OP.subtract),
                     reads=["stage1", "st2r", "st2n"], writes=["o_tok"],
                     dbg=f"u{u} ln2 norm"),
                 ["o_tok"], dbg=f"u{u} ln2")
        s.dma(lambda g, b=b, c=c: g.dma_start(
            out=out[b, c * 128:(c + 1) * 128, :], in_=o_tok[:, :]),
            reads=["o_tok"], dbg=f"st out u{u}", dsem="st0")

    chunks = [(c_ % nb, c_ // nb) for c_ in range(nb * nch)] if nb > 1 else \
        [(0, c_) for c_ in range(nch)]
    NN = len(chunks)

    xprep(0)
    front(0)
    if NN > 1:
        xprep(1)
    for t in range(NN):
        if t >= 2:
            back2a(t - 2)
        if t >= 1:
            back1b_act(t - 1)
        if t + 1 < NN:
            front(t + 1)
        if t >= 1:
            back1b_norm(t - 1)
            back1c(t - 1)
        if t >= 2:
            back2a_g(t - 2)
        back1a1(t)
        back1a2(t)
        backWp(t)
        if t + 2 < NN:
            xprep(t + 2)
        if t >= 2:
            back2b(t - 2)
        if t >= 2:
            back2c(t - 2)
    back1b_act(NN - 1)
    back1b_norm(NN - 1)
    back1c(NN - 1)
    for u in (NN - 2, NN - 1):
        back2a(u)
        back2a_g(u)
        back2b(u)
        back2c(u)

    s.finalize()
    sem_keys = ["tensor", "vector", "scalar", "pool", "sync",
                "su", "sid", "ws0", "ws1", "ws2", "sx0", "sx1", "sx2",
                "st0", "st1"]
    from contextlib import ExitStack as _ES
    with _ES() as sctx:
        sems = {k: sctx.enter_context(nc.semaphore(f"se_{k}"))
                for k in sem_keys}
        with nc.Block() as block:
            s.emit_all(block, sems)
    ctx.close()
    return nc


def make_in_maps(inputs, nb=B_PER_CORE, ncores=8):
    x = np.ascontiguousarray(inputs["L0_attn_x"], dtype=np.float32)
    brows = np.ascontiguousarray(np.stack([
        inputs["L0_attn_bq"], inputs["L0_attn_bk1"], inputs["L0_attn_bv"],
        inputs["L0_attn_bk2"], inputs["L0_attn_bp"], inputs["L0_ffn_bout"],
    ], axis=0).astype(np.float32))
    common = {
        "w_q": np.ascontiguousarray(inputs["L0_attn_Wq"], dtype=np.float32),
        "w_k1": np.ascontiguousarray(inputs["L0_attn_Wk1"], dtype=np.float32),
        "w_v": np.ascontiguousarray(inputs["L0_attn_Wv"], dtype=np.float32),
        "w_k2": np.ascontiguousarray(inputs["L0_attn_Wk2"], dtype=np.float32),
        "w_p": np.ascontiguousarray(inputs["L0_attn_Wp"], dtype=np.float32),
        "w_in": np.ascontiguousarray(inputs["L0_ffn_Win"], dtype=np.float32),
        "w_out": np.ascontiguousarray(inputs["L0_ffn_Wout"], dtype=np.float32),
        "w1d": np.ascontiguousarray(inputs["L0_ffn_W1"], dtype=np.float32),
        "b1d": np.ascontiguousarray(inputs["L0_ffn_b1"], dtype=np.float32),
        "w2d": np.ascontiguousarray(inputs["L0_ffn_W2"], dtype=np.float32),
        "b2d": np.ascontiguousarray(inputs["L0_ffn_b2"],
                                    dtype=np.float32).reshape(1, 64),
        "bind": np.ascontiguousarray(inputs["L0_ffn_bin_"],
                                     dtype=np.float32).reshape(1, 64),
        "brows": brows,
        "identd": np.eye(128, dtype=ml_dtypes.bfloat16),
        "maskd": np.triu(np.ones((128, 128))).astype(ml_dtypes.bfloat16),
    }
    in_maps = []
    for i in range(ncores):
        m = dict(common)
        m["xin"] = np.ascontiguousarray(x[nb * i: nb * (i + 1)])
        in_maps.append(m)
    return in_maps


def kernel(**inputs):
    nc = build_program()
    in_maps = make_in_maps(inputs)
    res = run_bass_kernel_spmd(nc, in_maps, core_ids=list(range(8)))
    return np.concatenate([r["out"] for r in res.results], axis=0)
